# version 04
# baseline (speedup 1.0000x reference)
import numpy as np
from contextlib import ExitStack

import concourse.bass as bass
import concourse.mybir as mybir
from concourse.bass_utils import run_bass_kernel_spmd

F32 = mybir.dt.float32
BF16 = mybir.dt.bfloat16
I8 = mybir.dt.int8
AF = mybir.ActivationFunctionType
AL = mybir.AluOpType

B, T, D = 4, 2048, 2048
HD, NH, INTER = 512, 3, 8192
FC = D // 128
NBK = T // 128
OWN = 8
TOK = OWN * 128
GCN = INTER // 512
EPS = 1e-5
N_BLOCKS = 4


def build_nc():
    nc = bass.Bass("TRN2", target_bir_lowering=False)
    ap = lambda t: t.ap()

    xoT = ap(nc.dram_tensor("xoT", [D, TOK], F32, kind="ExternalInput"))
    xsT = ap(nc.dram_tensor("xsT", [D, T], F32, kind="ExternalInput"))
    pofs = ap(nc.dram_tensor("pofs", [1], F32, kind="ExternalInput"))
    Wq = ap(nc.dram_tensor("Wq", [128, FC * HD], F32, kind="ExternalInput"))
    Wk = ap(nc.dram_tensor("Wk", [NH, 128, FC * HD], F32, kind="ExternalInput"))
    Wv = ap(nc.dram_tensor("Wv", [NH, 128, FC * HD], F32, kind="ExternalInput"))
    Wo = ap(nc.dram_tensor("Wo", [128, 4 * D], F32, kind="ExternalInput"))
    Wg = ap(nc.dram_tensor("Wg", [128, FC * NH], F32, kind="ExternalInput"))
    dlog = ap(nc.dram_tensor("dlog", [NH], F32, kind="ExternalInput"))
    osc = ap(nc.dram_tensor("osc", [1], F32, kind="ExternalInput"))
    wg8 = ap(nc.dram_tensor("wg8", [GCN, 128, 8192], I8, kind="ExternalInput"))
    wu8 = ap(nc.dram_tensor("wu8", [GCN, 128, 8192], I8, kind="ExternalInput"))
    wd8 = ap(nc.dram_tensor("wd8", [2, GCN, 128, 4096], I8, kind="ExternalInput"))
    sg = ap(nc.dram_tensor("sg", [INTER], F32, kind="ExternalInput"))
    su = ap(nc.dram_tensor("su", [INTER], F32, kind="ExternalInput"))
    sd = ap(nc.dram_tensor("sd", [D], F32, kind="ExternalInput"))
    W1 = ap(nc.dram_tensor("W1", [D, D], F32, kind="ExternalInput"))
    dW1 = ap(nc.dram_tensor("dW1", [D, D], F32, kind="ExternalInput"))
    g1 = ap(nc.dram_tensor("g1", [GCN, 128, 8192], F32, kind="ExternalInput"))
    u1 = ap(nc.dram_tensor("u1", [GCN, 128, 8192], F32, kind="ExternalInput"))
    d1 = ap(nc.dram_tensor("d1", [2, GCN, 128, 4096], F32, kind="ExternalInput"))
    yT = ap(nc.dram_tensor("yT", [D, TOK], F32, kind="ExternalOutput"))

    es = ExitStack()
    with es:
        sb = lambda name, shape, dt: es.enter_context(nc.sbuf_tensor(name, shape, dt))
        XR = sb("XR", [128, FC * 1024], F32)
        AR = sb("AR", [128, 40960], BF16)
        WG = sb("WG", [128, 16384], BF16)
        WD = sb("WD", [128, 8192], BF16)
        MS = sb("MS", [128, 16], F32)
        SCL = sb("SCL", [128, 144], F32)
        SRW = sb("SRW", [1, 1024], F32)
        GR = sb("GR", [3, 1024], F32)
        WGT = sb("WGT", [128, 48], BF16)
        ONER = sb("ONER", [1, 128], F32)
        ONE3 = sb("ONE3", [3, 1], F32)
        ONEC = sb("ONEC", [128, 1], BF16)
        E3 = sb("E3", [3, 384], F32)
        IDT = sb("IDT", [128, 128], BF16)
        MI = sb("MI", [128, 128], F32)
        PS = es.enter_context(nc.psum_tensor("PS", [128, 4096], F32))

        names = ["sq", "gq", "tm", "vc", "sc", "sw", "gp",
                 "ga", "gu", "gw0", "gw1", "sqa", "sqb"]
        SEM = {n: es.enter_context(nc.semaphore(n)) for n in names}
        C = {n: 0 for n in names}
        block = es.enter_context(nc.Block())

        def inc(inst, s, amt=1):
            C[s] += amt
            inst.then_inc(SEM[s], amt)
            return C[s]

        def w(eng, s, v):
            if v > 0:
                eng.wait_ge(SEM[s], v)

        MSc = lambda i: MS[:, i:i + 1]
        POFS, OSCC, LND0, EPSC = 0, 1, 2, 5

        def bank(i):
            return PS[:, 512 * i:512 * (i + 1)]

        def arf(c0, n):
            return AR[:, c0:c0 + 2 * n].bitcast(F32)

        def xrb(ic, t0, t1):
            return XR[:, ic * 1024 + t0 // 2: ic * 1024 + t1 // 2].bitcast(BF16)

        def setup_g(g):
            inc(g.iota(MI[:], [[-1, 128]], base=0, channel_multiplier=1,
                       allow_small_or_imprecise_dtypes=True), "gp")
            inc(g.memset(MSc(EPSC), EPS), "gp")
            inc(g.memset(ONER[:], 1.0), "gp")
            inc(g.memset(ONE3[:], 1.0), "gp")
            inc(g.memset(ONEC[:], 1.0), "gp")
            inc(g.iota(E3[:].rearrange("p (j m) -> p j m", m=128), [[-1, 3], [0, 128]],
                       base=0, channel_multiplier=1,
                       allow_small_or_imprecise_dtypes=True), "gp")
            inc(g.dma_start(out=MSc(POFS), in_=pofs.partition_broadcast(128)), "gq", 16)
            inc(g.dma_start(out=MSc(OSCC), in_=osc.partition_broadcast(128)), "gq", 16)
            for h in range(3):
                inc(g.dma_start(out=MSc(LND0 + h), in_=dlog[h:h + 1].partition_broadcast(128)), "gq", 16)
            inc(g.dma_start(out=SCL[:, 0:64].rearrange("p (g r) -> p g r", r=4),
                            in_=sg.rearrange("(g p r) -> p g r", p=128, r=4)), "gq", 16)
            inc(g.dma_start(out=SCL[:, 64:128].rearrange("p (g r) -> p g r", r=4),
                            in_=su.rearrange("(g p r) -> p g r", p=128, r=4)), "gq", 16)
            inc(g.dma_start(out=SCL[:, 128:144],
                            in_=sd.rearrange("(c p) -> p c", p=128),
                            allow_slow_non_contiguous=True), "sw", 16)
            inc(g.dma_start(out=WGT[:], in_=Wg[:, :]), "gq", 16)
        block.gpsimd(setup_g)
        GQ_SETUP = C["gq"]
        GP_SETUP = C["gp"]

        def setup_v(v):
            w(v, "gq", GQ_SETUP)
            w(v, "gp", GP_SETUP)
            inc(v.tensor_scalar(E3[:], E3[:], 0.0, None, op0=AL.is_equal), "vc")
            inc(v.tensor_scalar(IDT[:], MI[:], 0.0, None, op0=AL.is_equal), "vc")
        block.vector(setup_v)

        def setup_s(s):
            w(s, "gq", GQ_SETUP)
            w(s, "gp", GP_SETUP)
            for h in range(3):
                inc(s.activation(MSc(LND0 + h), MSc(LND0 + h), AF.Sigmoid), "sc")
        block.scalar(setup_s)
        SC_SIG = C["sc"]

        def setup_v2(v):
            w(v, "sc", SC_SIG)
            w(v, "sw", 16)
            for h in range(3):
                inc(v.tensor_scalar(MSc(LND0 + h), MSc(LND0 + h), 1e-38, None, op0=AL.max), "vc")
        block.vector(setup_v2)
        VC_CLAMP = C["vc"]

        def setup_s2(s):
            w(s, "vc", VC_CLAMP)
            for h in range(3):
                inc(s.activation(MSc(LND0 + h), MSc(LND0 + h), AF.Ln), "sc")
        block.scalar(setup_s2)
        SC_LND = C["sc"]

        def setup_t(t):
            w(t, "gp", GP_SETUP)
        block.tensor(setup_t)

        def emit_ln(ntok, dst, scr0, src=None, hsrc=None, hcol=0):
            ng = ntok // 512
            xb = [AR[:, scr0:scr0 + 512], AR[:, scr0 + 512:scr0 + 1024]]
            xq = [AR[:, scr0 + 1024:scr0 + 1536], AR[:, scr0 + 1536:scr0 + 2048]]
            stg = [arf(scr0 + 2048, 512), arf(scr0 + 3072, 512)]
            LNF1 = arf(scr0 + 4096, 512)
            LNF2 = arf(scr0 + 5120, 512)
            tm0, vc0, sc0 = C["tm"], C["vc"], C["sc"]

            for g in range(ng):
                t0 = g * 512
                tm_mms = {}
                vc_prep = {}
                sc_prep = {}
                sq_ld = {}
                for ic in range(FC):
                    bi = ic % 2
                    qs = "sqa" if bi == 0 else "sqb"
                    if hsrc is not None:
                        vw = vc_prep.get(ic - 2, C["vc"])
                        sw = sc_prep.get(ic - 2, C["sc"])

                        def ld(sy, ic=ic, bi=bi, t0=t0, vw=vw, sw=sw, qs=qs):
                            w(sy, "vc", vw)
                            w(sy, "sc", sw)
                            w(sy, "tm", tm0)
                            inc(sy.dma_start(out=stg[bi][:, 0:512],
                                             in_=hsrc[ic * 128:(ic + 1) * 128,
                                                      hcol + t0:hcol + t0 + 512]), qs, 16)
                        block.sync(ld)
                        sq_ld[ic] = C[qs]
                        srcap = stg[bi][:, 0:512]
                    else:
                        srcap = src(ic)[:, t0:t0 + 512]
                    tw = tm_mms.get(ic - 2, tm0)
                    qw = sq_ld.get(ic, 0)

                    def prep(v, srcap=srcap, bi=bi, tw=tw, qw=qw, qs=qs):
                        w(v, qs, qw)
                        w(v, "tm", tw)
                        w(v, "vc", C["vc"])
                        inc(v.tensor_copy(xb[bi], srcap), "vc")
                    block.vector(prep)
                    vc_prep[ic] = C["vc"]

                    def prepsq(s, srcap=srcap, bi=bi, tw=tw, qw=qw, qs=qs):
                        w(s, qs, qw)
                        w(s, "tm", tw)
                        w(s, "vc", vc0)
                        w(s, "sc", C["sc"])
                        inc(s.activation(xq[bi], srcap, AF.Square), "sc")
                    block.scalar(prepsq)
                    sc_prep[ic] = C["sc"]
                    vcn, scn = C["vc"], C["sc"]

                    def mms(t, ic=ic, bi=bi, vcn=vcn, scn=scn):
                        w(t, "vc", vcn)
                        w(t, "sc", scn)
                        inc(t.matmul(PS[0:1, 3072:3584], ONEC[:], xb[bi],
                                     start=(ic == 0), stop=(ic == FC - 1)), "tm")
                        inc(t.matmul(PS[0:1, 3584:4096], ONEC[:], xq[bi],
                                     start=(ic == 0), stop=(ic == FC - 1)), "tm")
                    block.tensor(mms)
                    tm_mms[ic] = C["tm"]
                tm_stats = C["tm"]

                def evac_rows(v):
                    w(v, "tm", tm_stats)
                    w(v, "vc", C["vc"])
                    inc(v.tensor_copy(SRW[0:1, 0:512], PS[0:1, 3072:3584]), "vc")
                    inc(v.tensor_copy(SRW[0:1, 512:1024], PS[0:1, 3584:4096]), "vc")
                block.vector(evac_rows)
                vc_rows = C["vc"]

                def bcasts(t):
                    w(t, "vc", vc_rows)
                    inc(t.matmul(bank(0), ONER[:], SRW[0:1, 0:512], start=True, stop=True), "tm")
                    inc(t.matmul(bank(1), ONER[:], SRW[0:1, 512:1024], start=True, stop=True), "tm")
                block.tensor(bcasts)
                tm_bc = C["tm"]

                def derive_v(v):
                    w(v, "tm", tm_bc)
                    w(v, "vc", C["vc"])
                    inc(v.tensor_scalar(LNF1, bank(0), -1.0 / D, None, op0=AL.mult), "vc")
                block.vector(derive_v)

                def derive_s(s):
                    w(s, "tm", tm_bc)
                    w(s, "vc", C["vc"] - 1)
                    w(s, "sc", C["sc"])
                    inc(s.activation(LNF2, bank(0), AF.Square, scale=1.0 / D), "sc")
                block.scalar(derive_s)
                sc_d = C["sc"]

                def derive_v2(v):
                    w(v, "sc", sc_d)
                    w(v, "vc", C["vc"])
                    inc(v.scalar_tensor_tensor(bank(3), bank(1), 1.0 / D, LNF2,
                                               op0=AL.mult, op1=AL.subtract), "vc")
                block.vector(derive_v2)
                vc_var = C["vc"]

                def derive_s2(s):
                    w(s, "vc", vc_var)
                    w(s, "sc", C["sc"])
                    inc(s.activation(bank(3), bank(3), AF.Sqrt, bias=MSc(EPSC)), "sc")
                block.scalar(derive_s2)
                sc_sd = C["sc"]

                def derive_v3(v):
                    w(v, "sc", sc_sd)
                    w(v, "vc", C["vc"])
                    inc(v.reciprocal(LNF2, bank(3)), "vc")
                block.vector(derive_v3)
                vc_inv = C["vc"]

                scA = C["sc"]
                vc_nrm = {}
                for ic in range(FC):
                    bi = ic % 2
                    qs = "sqa" if bi == 0 else "sqb"
                    if hsrc is not None:
                        vw = vc_nrm.get(ic - 2, vc_inv)

                        def ld2(sy, ic=ic, bi=bi, t0=t0, vw=vw, qs=qs):
                            w(sy, "vc", vw)
                            w(sy, "sc", scA)
                            inc(sy.dma_start(out=stg[bi][:, 0:512],
                                             in_=hsrc[ic * 128:(ic + 1) * 128,
                                                      hcol + t0:hcol + t0 + 512]), qs, 16)
                        block.sync(ld2)
                        sq2 = C[qs]
                        srcap = stg[bi][:, 0:512]
                    else:
                        sq2 = 0
                        srcap = src(ic)[:, t0:t0 + 512]

                    def nrm(v, srcap=srcap, ic=ic, g=g, sq2=sq2, qs=qs):
                        w(v, qs, sq2)
                        w(v, "tm", tm0)
                        w(v, "vc", C["vc"])
                        n1 = inc(v.tensor_tensor(bank(5), srcap, LNF1, op=AL.add), "vc")
                        v.wait_ge(SEM["vc"], n1)
                        inc(v.tensor_tensor(dst(ic, g), bank(5), LNF2, op=AL.mult), "vc")
                    block.vector(nrm)
                    vc_nrm[ic] = C["vc"]

        def emit_l0attn():
            A_XNH = 0
            A_Q = 8192
            A_K = 12288
            A_V = 20480
            A_MK = 28672
            A_MS1 = 30720
            A_MS3 = 30976
            A_CAU = 31232
            A_SC = 0
            A_GBC = 1024
            A_BL = 32768

            def ldwq(gp):
                inc(gp.dma_start(out=WG[:, 0:8192], in_=Wq[:, :]), "ga", 16)
            block.gpsimd(ldwq)
            gq_wq = C["ga"]

            for g in range(2):
                emit_ln(512,
                        lambda ic, gg, g=g: AR[:, A_XNH + ic * 512:A_XNH + ic * 512 + 512],
                        A_K, hsrc=xoT, hcol=g * 512)
                vc_xnh = C["vc"]

                for dc in range(4):
                    bq = dc % 2
                    scw = C["sc"] - 1 if dc >= 2 else C["sc"]

                    def qmm(t, dc=dc, bq=bq, vc_xnh=vc_xnh, scw=scw):
                        w(t, "vc", vc_xnh)
                        w(t, "ga", gq_wq)
                        w(t, "sc", scw)
                        for ic in range(FC):
                            inc(t.matmul(bank(bq), WG[:, ic * 512 + dc * 128:ic * 512 + dc * 128 + 128],
                                         AR[:, A_XNH + ic * 512:A_XNH + ic * 512 + 512],
                                         start=(ic == 0), stop=(ic == FC - 1)), "tm")
                    block.tensor(qmm)
                    tmq = C["tm"]

                    def qev(s, dc=dc, bq=bq, g=g, tmq=tmq):
                        w(s, "tm", tmq)
                        w(s, "sc", C["sc"])
                        inc(s.activation(AR[:, A_Q + dc * 1024 + g * 512:A_Q + dc * 1024 + g * 512 + 512],
                                         bank(bq), AF.Copy), "sc")
                    block.scalar(qev)

                vcw_g = C["vc"]
                scw_g = C["sc"]

                def gmm(t, vcw_g=vcw_g):
                    w(t, "vc", vcw_g)
                    w(t, "sc", scw_g)
                    for ic in range(FC):
                        inc(t.matmul(PS[0:3, 1024:1536], WGT[:, ic * 3:ic * 3 + 3],
                                     AR[:, A_XNH + ic * 512:A_XNH + ic * 512 + 512],
                                     start=(ic == 0), stop=(ic == FC - 1)), "tm")
                block.tensor(gmm)
                tmg = C["tm"]

                def gev(v, g=g, tmg=tmg):
                    w(v, "tm", tmg)
                    w(v, "vc", C["vc"])
                    inc(v.tensor_copy(GR[0:3, g * 512:g * 512 + 512], PS[0:3, 1024:1536]), "vc")
                block.vector(gev)

            vc_g = C["vc"]

            def gsm_t1(t):
                w(t, "vc", vc_g)
                inc(t.matmul(PS[0:1, 3072:3584], ONE3[:], GR[0:3, 0:512], start=True, stop=True), "tm")
                inc(t.matmul(PS[0:1, 3584:4096], ONE3[:], GR[0:3, 512:1024], start=True, stop=True), "tm")
            block.tensor(gsm_t1)
            tm_gs = C["tm"]

            def gsm_v1(v):
                w(v, "tm", tm_gs)
                w(v, "vc", C["vc"])
                inc(v.tensor_scalar(PS[0:1, 3072:3584], PS[0:1, 3072:3584], -1.0 / 3.0, None, op0=AL.mult), "vc")
                inc(v.tensor_scalar(PS[0:1, 3584:4096], PS[0:1, 3584:4096], -1.0 / 3.0, None, op0=AL.mult), "vc")
                v.wait_ge(SEM["vc"], C["vc"])
                inc(v.tensor_copy(SRW[0:1, 0:512], PS[0:1, 3072:3584]), "vc")
                inc(v.tensor_copy(SRW[0:1, 512:1024], PS[0:1, 3584:4096]), "vc")
            block.vector(gsm_v1)
            vc_gs = C["vc"]

            def gsm_t2(t):
                w(t, "vc", vc_gs)
                inc(t.matmul(PS[0:3, 1024:1536], ONER[0:1, 0:3], SRW[0:1, 0:512], start=True, stop=True), "tm")
                inc(t.matmul(PS[0:3, 1536:2048], ONER[0:1, 0:3], SRW[0:1, 512:1024], start=True, stop=True), "tm")
            block.tensor(gsm_t2)
            tm_gs2 = C["tm"]

            def gsm_v2(v):
                w(v, "tm", tm_gs2)
                w(v, "vc", C["vc"])
                inc(v.tensor_tensor(GR[0:3, :], GR[0:3, :], PS[0:3, 1024:2048], op=AL.add), "vc")
            block.vector(gsm_v2)
            vc_gs2 = C["vc"]

            def gsm_s(s):
                w(s, "vc", vc_gs2)
                w(s, "sc", C["sc"])
                inc(s.activation(GR[0:3, :], GR[0:3, :], AF.Exp), "sc")
            block.scalar(gsm_s)
            sc_gs = C["sc"]

            def gsm_t3(t):
                w(t, "sc", sc_gs)
                inc(t.matmul(PS[0:1, 3072:3584], ONE3[:], GR[0:3, 0:512], start=True, stop=True), "tm")
                inc(t.matmul(PS[0:1, 3584:4096], ONE3[:], GR[0:3, 512:1024], start=True, stop=True), "tm")
            block.tensor(gsm_t3)
            tm_gs3 = C["tm"]

            def gsm_v3(v):
                w(v, "tm", tm_gs3)
                w(v, "vc", C["vc"])
                inc(v.reciprocal(SRW[0:1, 0:512], PS[0:1, 3072:3584]), "vc")
                inc(v.reciprocal(SRW[0:1, 512:1024], PS[0:1, 3584:4096]), "vc")
            block.vector(gsm_v3)
            vc_gs3 = C["vc"]

            def gsm_t4(t):
                w(t, "vc", vc_gs3)
                inc(t.matmul(PS[0:3, 1024:1536], ONER[0:1, 0:3], SRW[0:1, 0:512], start=True, stop=True), "tm")
                inc(t.matmul(PS[0:3, 1536:2048], ONER[0:1, 0:3], SRW[0:1, 512:1024], start=True, stop=True), "tm")
            block.tensor(gsm_t4)
            tm_gs4 = C["tm"]

            def gsm_v4(v):
                w(v, "tm", tm_gs4)
                w(v, "vc", C["vc"])
                inc(v.tensor_tensor(GR[0:3, :], GR[0:3, :], PS[0:3, 1024:2048], op=AL.mult), "vc")
            block.vector(gsm_v4)

            emit_ln(T, lambda ic, g: xrb(ic, g * 512, g * 512 + 512),
                    A_V, hsrc=xsT, hcol=0)
            VC_XNSEQ = C["vc"]

            tm_head0 = C["tm"]
            tm_kv_done = [0, 0, 0]
            for h in range(3):
                tkw = tm_kv_done[h - 1] if h >= 1 else tm_head0

                def ldk(gp, h=h, tkw=tkw):
                    w(gp, "tm", tkw)
                    inc(gp.dma_start(out=WG[:, 0:8192], in_=Wk[h]), "ga", 16)
                block.gpsimd(ldk)
                gq_k = C["ga"]

                def ldv(gp, h=h, tkw=tkw):
                    w(gp, "tm", tkw)
                    inc(gp.dma_start(out=WG[:, 8192:16384], in_=Wv[h]), "gu", 16)
                block.gpsimd(ldv)
                gq_v = C["gu"]

                for dc in range(4):
                    for sg_ in range(4):
                        idx = dc * 4 + sg_
                        bq = idx % 2
                        scw = C["sc"] - 1 if idx >= 2 else C["sc"]

                        def kmm(t, dc=dc, sg_=sg_, bq=bq, idx=idx, scw=scw):
                            if idx == 0:
                                w(t, "vc", VC_XNSEQ)
                                w(t, "ga", gq_k)
                            w(t, "sc", scw)
                            for ic in range(FC):
                                inc(t.matmul(bank(bq),
                                             WG[:, ic * 512 + dc * 128:ic * 512 + dc * 128 + 128],
                                             xrb(ic, sg_ * 512, sg_ * 512 + 512),
                                             start=(ic == 0), stop=(ic == FC - 1)), "tm")
                        block.tensor(kmm)
                        tmk = C["tm"]

                        def kev(s, dc=dc, sg_=sg_, bq=bq, tmk=tmk):
                            w(s, "tm", tmk)
                            w(s, "sc", C["sc"])
                            inc(s.activation(AR[:, A_K + dc * 2048 + sg_ * 512:
                                                A_K + dc * 2048 + sg_ * 512 + 512],
                                             bank(bq), AF.Copy), "sc")
                        block.scalar(kev)
                for j in range(NBK):
                    bq = j % 2
                    scw = C["sc"] - 1 if j >= 2 else C["sc"]

                    def vmm(t, j=j, bq=bq, scw=scw):
                        if j == 0:
                            w(t, "gu", gq_v)
                        w(t, "sc", scw)
                        for ic in range(FC):
                            inc(t.matmul(bank(bq),
                                         xrb(ic, j * 128, j * 128 + 128),
                                         WG[:, 8192 + ic * 512:8192 + ic * 512 + 512],
                                         start=(ic == 0), stop=(ic == FC - 1)), "tm")
                    block.tensor(vmm)
                    tmv = C["tm"]

                    def vev(s, j=j, bq=bq, tmv=tmv):
                        w(s, "tm", tmv)
                        w(s, "sc", C["sc"])
                        inc(s.activation(AR[:, A_V + j * 512:A_V + j * 512 + 512],
                                         bank(bq), AF.Copy), "sc")
                    block.scalar(vev)
                tm_kv_done[h] = C["tm"]

                msf = arf(A_MS1, 128)
                mse = arf(A_MS3, 128)
                for dp in range(16):
                    scw = max(SC_LND, C["sc"] - 1 if dp >= 1 else 0)

                    def mk1(v, dp=dp, scw=scw):
                        w(v, "sc", scw)
                        w(v, "vc", C["vc"])
                        m1 = inc(v.tensor_scalar(msf, MI[:], 128.0 * dp - 1.0, MSc(POFS),
                                                 op0=AL.add, op1=AL.subtract), "vc")
                        v.wait_ge(SEM["vc"], m1)
                        inc(v.tensor_scalar(AR[:, A_CAU:A_CAU + 128], msf, 0.0, None,
                                            op0=AL.is_ge), "vc")
                    block.vector(mk1)
                    vcm1 = C["vc"]

                    def mk1b(v, h=h, vcm1=vcm1):
                        w(v, "vc", vcm1)
                        inc(v.tensor_scalar(msf, msf, MSc(LND0 + h), 0.0,
                                            op0=AL.mult, op1=AL.min), "vc")
                    block.vector(mk1b)
                    vcm = C["vc"]

                    def mk2(s, vcm=vcm):
                        w(s, "vc", vcm)
                        w(s, "sc", C["sc"])
                        inc(s.activation(mse, msf, AF.Exp), "sc")
                    block.scalar(mk2)
                    scm = C["sc"]

                    def mk3(v, dp=dp, scm=scm):
                        w(v, "sc", scm)
                        w(v, "vc", C["vc"])
                        inc(v.tensor_tensor(AR[:, A_MK + dp * 128:A_MK + dp * 128 + 128],
                                            mse, AR[:, A_CAU:A_CAU + 128], op=AL.mult), "vc")
                    block.vector(mk3)
                vc_masks = C["vc"]

                def gbc(t, h=h, vc_masks=vc_masks):
                    w(t, "vc", vc_masks)
                    inc(t.matmul(bank(6), E3[:, h * 128:h * 128 + 128], GR[0:3, 0:512],
                                 start=True, stop=True), "tm")
                    inc(t.matmul(bank(7), E3[:, h * 128:h * 128 + 128], GR[0:3, 512:1024],
                                 start=True, stop=True), "tm")
                block.tensor(gbc)
                tm_gbc = C["tm"]

                def gbcev(v, tm_gbc=tm_gbc):
                    w(v, "tm", tm_gbc)
                    w(v, "vc", C["vc"])
                    gb = arf(A_GBC, 1024)
                    inc(v.tensor_copy(gb[:, 0:512], bank(6)), "vc")
                    inc(v.tensor_copy(gb[:, 512:1024], bank(7)), "vc")
                block.vector(gbcev)
                vc_gbc = C["vc"]

                vc_rev = {}
                for j in range(NBK):
                    nj = j // 2 + 1
                    Nj = nj * 128
                    npieces = (Nj + 511) // 512
                    vcw = C["vc"]

                    def scmm(t, j=j, Nj=Nj, npieces=npieces, vcw=vcw):
                        w(t, "vc", max(vcw, vc_gbc if j == 0 else 0))
                        for pc in range(npieces):
                            n0 = pc * 512
                            n1 = min(Nj, n0 + 512)
                            for dc in range(4):
                                inc(t.matmul(PS[:, n0:n1],
                                             AR[:, A_K + dc * 2048 + j * 128:A_K + dc * 2048 + j * 128 + 128],
                                             AR[:, A_Q + dc * 1024 + n0:A_Q + dc * 1024 + n1],
                                             start=(dc == 0), stop=(dc == 3)), "tm")
                    block.tensor(scmm)
                    tm_sc = C["tm"]

                    def mmul(v, j=j, nj=nj, tm_sc=tm_sc):
                        w(v, "tm", tm_sc)
                        w(v, "vc", C["vc"])
                        for bi in range(nj):
                            dp = j - 2 * bi
                            inc(v.tensor_tensor(AR[:, A_SC + bi * 128:A_SC + bi * 128 + 128],
                                                PS[:, bi * 128:bi * 128 + 128],
                                                AR[:, A_MK + dp * 128:A_MK + dp * 128 + 128],
                                                op=AL.mult), "vc")
                    block.vector(mmul)
                    vc_mm = C["vc"]

                    for dc in range(4):
                        rb = 2 + 2 * (dc % 2)
                        ridx = j * 4 + dc
                        rvw = vc_rev.get(ridx - 2, 0)

                        def rmm(t, j=j, Nj=Nj, dc=dc, rb=rb, vc_mm=vc_mm, rvw=rvw):
                            w(t, "vc", max(vc_mm, rvw))
                            for pc in range((Nj + 511) // 512):
                                n0 = pc * 512
                                n1 = min(Nj, n0 + 512)
                                inc(t.matmul(PS[:, rb * 512 + n0:rb * 512 + n1],
                                             AR[:, A_V + j * 512 + dc * 128:A_V + j * 512 + dc * 128 + 128],
                                             AR[:, A_SC + n0:A_SC + n1],
                                             start=True, stop=True), "tm")
                        block.tensor(rmm)
                        tm_r = C["tm"]

                        def rev(v, j=j, Nj=Nj, dc=dc, rb=rb, h=h, tm_r=tm_r):
                            w(v, "tm", tm_r)
                            w(v, "vc", C["vc"])
                            blv = arf(A_BL + dc * 2048, 1024)
                            gb = arf(A_GBC, 1024)
                            r1 = inc(v.tensor_tensor(PS[:, rb * 512:rb * 512 + Nj],
                                                     PS[:, rb * 512:rb * 512 + Nj],
                                                     gb[:, 0:Nj], op=AL.mult), "vc")
                            v.wait_ge(SEM["vc"], r1)
                            if h == 0 and j % 2 == 0:
                                new0 = Nj - 128
                                if new0 > 0:
                                    inc(v.tensor_tensor(blv[:, 0:new0], blv[:, 0:new0],
                                                        PS[:, rb * 512:rb * 512 + new0], op=AL.add), "vc")
                                inc(v.tensor_copy(blv[:, new0:Nj], PS[:, rb * 512 + new0:rb * 512 + Nj]), "vc")
                            else:
                                inc(v.tensor_tensor(blv[:, 0:Nj], blv[:, 0:Nj],
                                                    PS[:, rb * 512:rb * 512 + Nj], op=AL.add), "vc")
                        block.vector(rev)
                        vc_rev[ridx] = C["vc"]

            vc_ret = C["vc"]

            def blb(v):
                w(v, "vc", vc_ret)
                for dc in range(4):
                    inc(v.tensor_copy(AR[:, A_Q + dc * 1024:A_Q + dc * 1024 + 1024],
                                      arf(A_BL + dc * 2048, 1024)), "vc")
            block.vector(blb)
            vc_blb = C["vc"]

            def ldwo(gp):
                w(gp, "tm", tm_kv_done[2])
                inc(gp.dma_start(out=WD[:], in_=Wo[:, :]), "gu", 16)
            block.gpsimd(ldwo)
            gq_wo = C["gu"]

            xobuf = [arf(A_K, 512), arf(A_K + 1024, 512)]
            vc_woev = {}
            for oc in range(FC):
                for th in range(2):
                    idx = oc * 2 + th
                    bi = idx % 2
                    lvw = vc_woev.get(idx - 2, vc_blb)

                    xqs = "sqa" if bi == 0 else "sqb"

                    def ldxo(sy, oc=oc, th=th, bi=bi, lvw=lvw, xqs=xqs):
                        w(sy, "vc", lvw)
                        inc(sy.dma_start(out=xobuf[bi][:, 0:512],
                                         in_=xoT[oc * 128:(oc + 1) * 128, th * 512:th * 512 + 512]), xqs, 16)
                    block.sync(ldxo)
                    sq_xo = C[xqs]
                    bq = idx % 2
                    wvw = vc_woev.get(idx - 2, vc_blb)

                    def womm(t, oc=oc, th=th, bq=bq, idx=idx, wvw=wvw):
                        if idx == 0:
                            w(t, "vc", vc_blb)
                            w(t, "gu", gq_wo)
                            w(t, "sc", C["sc"])
                        w(t, "vc", wvw)
                        for dc in range(4):
                            inc(t.matmul(bank(bq),
                                         WD[:, dc * 2048 + oc * 128:dc * 2048 + oc * 128 + 128],
                                         AR[:, A_Q + dc * 1024 + th * 512:A_Q + dc * 1024 + th * 512 + 512],
                                         start=(dc == 0), stop=(dc == 3)), "tm")
                    block.tensor(womm)
                    tm_wo = C["tm"]

                    def woev(v, oc=oc, th=th, bq=bq, bi=bi, tm_wo=tm_wo, sq_xo=sq_xo):
                        w(v, "tm", tm_wo)
                        w(v, xqs, sq_xo)
                        w(v, "vc", C["vc"])
                        inc(v.scalar_tensor_tensor(
                            XR[:, oc * 1024 + th * 512:oc * 1024 + th * 512 + 512],
                            bank(bq), MSc(OSCC), xobuf[bi][:, 0:512],
                            op0=AL.mult, op1=AL.add), "vc")
                    block.vector(woev)
                    vc_woev[idx] = C["vc"]

        def emit_ffn(layer, wg_free_tm, wd_free_tm):
            A_XN = 0
            A_H = 8192
            A_LNS = A_H
            wq_, uq_, dq_ = (wg8, wu8, wd8) if layer == 0 else (g1, u1, d1)
            st = {"tm_gate": wg_free_tm, "tm_up": wg_free_tm,
                  "tm_down": [wd_free_tm, wd_free_tm]}

            for tt in range(2):
                emit_ln(512,
                        lambda ic, g: AR[:, A_XN + ic * 512:A_XN + ic * 512 + 512],
                        A_LNS,
                        src=lambda ic, tt=tt: XR[:, ic * 1024 + tt * 512:ic * 1024 + tt * 512 + 512])
                vc_xn = C["vc"]

                vc_fuse_prev = 0
                for gc in range(GCN):
                    tgw = st["tm_gate"]

                    def ldg(gp, gc=gc, tgw=tgw):
                        w(gp, "tm", tgw)
                        inc(gp.dma_start(out=WG[:, 0:8192], in_=wq_[gc]), "ga", 16)
                    block.gpsimd(ldg)
                    gq_g = C["ga"]
                    tuw = st["tm_up"]

                    def ldu(gp, gc=gc, tuw=tuw):
                        w(gp, "tm", tuw)
                        inc(gp.dma_start(out=WG[:, 8192:16384], in_=uq_[gc]), "gu", 16)
                    block.gpsimd(ldu)
                    gq_u = C["gu"]

                    def gatemm(t, gc=gc, gq_g=gq_g, vcp=vc_fuse_prev, vcx=vc_xn):
                        w(t, "ga", gq_g)
                        w(t, "vc", max(vcp, vcx))
                        for r in range(4):
                            for ic in range(FC):
                                inc(t.matmul(PS[:, r * 512:(r + 1) * 512],
                                             WG[:, ic * 512 + r:ic * 512 + 512:4],
                                             AR[:, A_XN + ic * 512:A_XN + ic * 512 + 512],
                                             start=(ic == 0), stop=(ic == FC - 1)), "tm")
                    block.tensor(gatemm)
                    st["tm_gate"] = C["tm"]
                    tm_gate = C["tm"]

                    def upmm(t, gc=gc, gq_u=gq_u, vcp=vc_fuse_prev):
                        w(t, "gu", gq_u)
                        w(t, "vc", vcp)
                        for r in range(4):
                            for ic in range(FC):
                                inc(t.matmul(PS[:, 2048 + r * 512:2048 + (r + 1) * 512],
                                             WG[:, 8192 + ic * 512 + r:8192 + ic * 512 + 512:4],
                                             AR[:, A_XN + ic * 512:A_XN + ic * 512 + 512],
                                             start=(ic == 0), stop=(ic == FC - 1)), "tm")
                    block.tensor(upmm)
                    st["tm_up"] = C["tm"]
                    tm_up = C["tm"]

                    if layer == 0:
                        def siluev(s, gc=gc, tm_up=tm_up):
                            w(s, "tm", tm_up)
                            w(s, "sc", C["sc"])
                            for r in range(4):
                                inc(s.activation(
                                    AR[:, A_H + gc * 2048 + r * 512:A_H + gc * 2048 + (r + 1) * 512],
                                    PS[:, r * 512:(r + 1) * 512], AF.Silu,
                                    scale=SCL[:, gc * 4 + r:gc * 4 + r + 1]), "sc")
                                inc(s.activation(PS[:, 2048 + r * 512:2048 + (r + 1) * 512],
                                                 PS[:, 2048 + r * 512:2048 + (r + 1) * 512], AF.Copy,
                                                 scale=SCL[:, 64 + gc * 4 + r:64 + gc * 4 + r + 1]), "sc")
                        block.scalar(siluev)
                        sc_silu = C["sc"]

                        def fuse(v, gc=gc, sc_silu=sc_silu):
                            w(v, "sc", sc_silu)
                            w(v, "vc", C["vc"])
                            for r in range(4):
                                hs = AR[:, A_H + gc * 2048 + r * 512:A_H + gc * 2048 + (r + 1) * 512]
                                inc(v.tensor_tensor(hs, hs,
                                                    PS[:, 2048 + r * 512:2048 + (r + 1) * 512],
                                                    op=AL.mult), "vc")
                        block.vector(fuse)
                    else:
                        HG = WD[:, 0:1024].bitcast(F32)
                        keep = WD[:, 0:512]
                        ABS = lambda r: WD[:, 1024 + r * 1024:1024 + (r + 1) * 1024].bitcast(F32)
                        T1 = WD[:, 5120:6144].bitcast(F32)
                        T2 = WD[:, 6144:7168].bitcast(F32)
                        T3 = WD[:, 7168:8192].bitcast(F32)
                        for r in range(4):
                            svw = C["vc"]

                            def silur(s, r=r, tm_up=tm_up, svw=svw):
                                w(s, "tm", tm_up)
                                w(s, "vc", svw)
                                w(s, "sc", C["sc"])
                                inc(s.activation(HG, PS[:, r * 512:(r + 1) * 512], AF.Silu), "sc")
                            block.scalar(silur)
                            sc_r = C["sc"]

                            def hr(v, r=r, sc_r=sc_r):
                                w(v, "sc", sc_r)
                                w(v, "vc", C["vc"])
                                inc(v.tensor_tensor(PS[:, r * 512:(r + 1) * 512], HG,
                                                    PS[:, r * 512:(r + 1) * 512],
                                                    op=AL.mult), "vc")
                            block.vector(hr)
                        vc_h = C["vc"]

                        def absf(s, vc_h=vc_h):
                            w(s, "vc", vc_h)
                            w(s, "sc", C["sc"])
                            for r in range(4):
                                inc(s.activation(ABS(r), PS[:, r * 512:(r + 1) * 512], AF.Abs), "sc")
                        block.scalar(absf)
                        sc_abs = C["sc"]

                        def mask(v, gc=gc, sc_abs=sc_abs):
                            w(v, "sc", sc_abs)
                            def step(op):
                                v.wait_ge(SEM["vc"], C["vc"])
                                inc(op(), "vc")
                            step(lambda: v.tensor_tensor(T1, ABS(0), ABS(1), op=AL.max))
                            step(lambda: v.tensor_tensor(T2, ABS(2), ABS(3), op=AL.max))
                            step(lambda: v.tensor_tensor(T1, T1, T2, op=AL.min))
                            step(lambda: v.tensor_tensor(T2, ABS(0), ABS(1), op=AL.min))
                            step(lambda: v.tensor_tensor(T3, ABS(2), ABS(3), op=AL.min))
                            step(lambda: v.tensor_tensor(T2, T2, T3, op=AL.max))
                            step(lambda: v.tensor_tensor(T1, T1, T2, op=AL.max))
                            for r in range(4):
                                step(lambda r=r: v.tensor_tensor(keep, ABS(r), T1, op=AL.is_ge))
                                step(lambda r=r: v.tensor_tensor(
                                    AR[:, A_H + gc * 2048 + r * 512:A_H + gc * 2048 + (r + 1) * 512],
                                    PS[:, r * 512:(r + 1) * 512], keep, op=AL.mult))
                        block.vector(mask)
                    vc_fuse_prev = C["vc"]
                vc_p1 = C["vc"]

                for dmh in range(2):
                    for gc in range(GCN):
                        wbuf = (gc % 2) * 4096
                        tdw = st["tm_down"][gc % 2]
                        mvw = vc_p1 if (layer == 1 and dmh == 0 and gc < 2) else 0

                        gws = "gw0" if gc % 2 == 0 else "gw1"

                        def ldd(gp, gc=gc, dmh=dmh, wbuf=wbuf, tdw=tdw, mvw=mvw, gws=gws):
                            w(gp, "tm", tdw)
                            w(gp, "vc", mvw)
                            inc(gp.dma_start(out=WD[:, wbuf:wbuf + 4096],
                                             in_=dq_[dmh, gc]), gws, 16)
                        block.gpsimd(ldd)
                        gq_d = C[gws]
                        dvw = C["vc"]

                        def downmm(t, gc=gc, wbuf=wbuf, gq_d=gq_d, dvw=dvw, gws=gws):
                            w(t, gws, gq_d)
                            if gc == 0:
                                w(t, "vc", max(vc_p1, dvw))
                                w(t, "sc", C["sc"])
                            for r in range(4):
                                for oc in range(8):
                                    inc(t.matmul(PS[:, oc * 512:(oc + 1) * 512],
                                                 WD[:, wbuf + r * 1024 + oc * 128:
                                                    wbuf + r * 1024 + oc * 128 + 128],
                                                 AR[:, A_H + gc * 2048 + r * 512:
                                                    A_H + gc * 2048 + (r + 1) * 512],
                                                 start=(gc == 0 and r == 0),
                                                 stop=(gc == GCN - 1 and r == 3)), "tm")
                        block.tensor(downmm)
                        st["tm_down"][gc % 2] = C["tm"]
                    tm_down = C["tm"]

                    def devac(v, dmh=dmh, tt=tt, layer=layer, tm_down=tm_down):
                        w(v, "tm", tm_down)
                        w(v, "vc", C["vc"])
                        for oc in range(8):
                            goc = dmh * 8 + oc
                            dst = XR[:, goc * 1024 + tt * 512:goc * 1024 + tt * 512 + 512]
                            if layer == 0:
                                inc(v.scalar_tensor_tensor(dst, PS[:, oc * 512:(oc + 1) * 512],
                                                           SCL[:, 128 + goc:128 + goc + 1], dst,
                                                           op0=AL.mult, op1=AL.add), "vc")
                            else:
                                inc(v.tensor_tensor(dst, PS[:, oc * 512:(oc + 1) * 512], dst,
                                                    op=AL.add), "vc")
                    block.vector(devac)

        def emit_l1attn(wg_free_vc, wd_free_tm):
            A_XN = 0
            A_LNS = 16384
            emit_ln(TOK,
                    lambda ic, g: AR[:, A_XN + ic * 1024 + g * 512:A_XN + ic * 1024 + g * 512 + 512],
                    A_LNS,
                    src=lambda ic: XR[:, ic * 1024:(ic + 1) * 1024])
            vc_xn3 = C["vc"]

            wgf = WG[:].bitcast(F32)
            vc_msum = [wg_free_vc, wg_free_vc]
            tm_tr = [wd_free_tm, wd_free_tm]
            vc_yev = {}
            for oc in range(FC):
                bsel = oc % 2
                lvw = vc_msum[bsel]

                wqs = "sqa" if bsel == 0 else "sqb"

                def ldw(sy, oc=oc, bsel=bsel, lvw=lvw, wqs=wqs):
                    w(sy, "vc", lvw)
                    inc(sy.dma_start(out=wgf[:, bsel * 2048:bsel * 2048 + 2048],
                                     in_=W1[oc * 128:(oc + 1) * 128, :]), wqs, 16)
                    inc(sy.dma_start(out=wgf[:, 4096 + bsel * 2048:4096 + bsel * 2048 + 2048],
                                     in_=dW1[oc * 128:(oc + 1) * 128, :]), wqs, 16)
                block.sync(ldw)
                sq_w = C[wqs]
                mtw = tm_tr[bsel]

                def msum(v, bsel=bsel, sq_w=sq_w, mtw=mtw, wqs=wqs):
                    w(v, wqs, sq_w)
                    w(v, "tm", mtw)
                    w(v, "vc", C["vc"])
                    inc(v.tensor_tensor(WD[:, bsel * 2048:bsel * 2048 + 2048],
                                        wgf[:, bsel * 2048:bsel * 2048 + 2048],
                                        wgf[:, 4096 + bsel * 2048:4096 + bsel * 2048 + 2048],
                                        op=AL.add), "vc")
                block.vector(msum)
                vc_msum[bsel] = C["vc"]
                vc_ms = C["vc"]

                tdst = 4096 + bsel * 2048
                vc_trev = {}
                for grp in range(4):
                    tb = grp % 2
                    tvw = vc_trev.get(grp - 2, 0)

                    def trmm(t, grp=grp, tb=tb, bsel=bsel, vc_ms=vc_ms, tvw=tvw):
                        w(t, "vc", max(vc_ms, tvw))
                        for k in range(4):
                            ic = grp * 4 + k
                            inc(t.matmul(PS[:, tb * 512 + k * 128:tb * 512 + k * 128 + 128],
                                         WD[:, bsel * 2048 + ic * 128:bsel * 2048 + ic * 128 + 128],
                                         IDT[:], start=True, stop=True), "tm")
                    block.tensor(trmm)
                    tm_t = C["tm"]

                    def trev(v, grp=grp, tb=tb, tdst=tdst, tm_t=tm_t):
                        w(v, "tm", tm_t)
                        w(v, "vc", C["vc"])
                        inc(v.tensor_copy(WD[:, tdst + grp * 512:tdst + grp * 512 + 512],
                                          PS[:, tb * 512:tb * 512 + 512]), "vc")
                    block.vector(trev)
                    vc_trev[grp] = C["vc"]
                tm_tr[bsel] = C["tm"]
                vc_tr = C["vc"]

                for th in range(2):
                    idx = oc * 2 + th
                    yb = 2 + idx % 2
                    yvw = vc_yev.get(idx - 2, 0)

                    def ymm(t, oc=oc, th=th, yb=yb, tdst=tdst, vc_tr=vc_tr, yvw=yvw):
                        w(t, "vc", max(vc_tr, vc_xn3, yvw))
                        for ic in range(FC):
                            inc(t.matmul(PS[:, yb * 512:yb * 512 + 512],
                                         WD[:, tdst + ic * 128:tdst + ic * 128 + 128],
                                         AR[:, A_XN + ic * 1024 + th * 512:A_XN + ic * 1024 + th * 512 + 512],
                                         start=(ic == 0), stop=(ic == FC - 1)), "tm")
                    block.tensor(ymm)
                    tm_y = C["tm"]

                    def yev(v, oc=oc, th=th, yb=yb, tm_y=tm_y):
                        w(v, "tm", tm_y)
                        w(v, "vc", C["vc"])
                        dst = XR[:, oc * 1024 + th * 512:oc * 1024 + th * 512 + 512]
                        inc(v.tensor_tensor(dst, PS[:, yb * 512:yb * 512 + 512], dst,
                                            op=AL.add), "vc")
                    block.vector(yev)
                    vc_yev[idx] = C["vc"]

        if N_BLOCKS >= 1:
            emit_l0attn()
        if N_BLOCKS >= 2:
            emit_ffn(0, wg_free_tm=C["tm"], wd_free_tm=C["tm"])
        if N_BLOCKS >= 3:
            emit_l1attn(wg_free_vc=C["vc"], wd_free_tm=C["tm"])
        if N_BLOCKS >= 4:
            emit_ffn(1, wg_free_tm=C["tm"], wd_free_tm=C["tm"])

        vc_final = C["vc"]

        def store(sy):
            w(sy, "vc", vc_final)
            for ic in range(FC):
                inc(sy.dma_start(out=yT[ic * 128:(ic + 1) * 128, :],
                                 in_=XR[:, ic * 1024:(ic + 1) * 1024]), "sq", 16)
            sy.wait_ge(SEM["sq"], C["sq"])
        block.sync(store)

    return nc


def host_base(inputs):
    def pre_cm(wm):
        Cc, M = wm.shape[0] // 128, wm.shape[1]
        return np.ascontiguousarray(wm.reshape(Cc, 128, M).transpose(1, 0, 2).reshape(128, Cc * M))

    def pre_gc(wm):
        wr = wm.reshape(16, 128, 16, 512)
        return np.ascontiguousarray(wr.transpose(2, 1, 0, 3).reshape(16, 128, 8192))

    def pre_dn(wm):
        wr = wm.reshape(16, 128, 4, 2, 1024)
        return np.ascontiguousarray(wr.transpose(3, 0, 1, 2, 4).reshape(2, 16, 128, 4096))

    base = {
        "Wq": pre_cm(np.asarray(inputs["L0_attn_Wq"])),
        "Wk": np.stack([pre_cm(np.asarray(inputs["L0_attn_Wk"])[h]) for h in range(3)]),
        "Wv": np.stack([pre_cm(np.asarray(inputs["L0_attn_Wv"])[h]) for h in range(3)]),
        "Wo": pre_cm(np.asarray(inputs["L0_attn_Wo"])),
        "Wg": pre_cm(np.asarray(inputs["L0_attn_Wg"])),
        "dlog": np.asarray(inputs["L0_attn_decay_logits"]),
        "osc": np.asarray(inputs["L0_attn_out_scale"]).reshape(1),
        "wg8": pre_gc(np.asarray(inputs["L0_ffn_w_gate"])),
        "wu8": pre_gc(np.asarray(inputs["L0_ffn_w_up"])),
        "wd8": pre_dn(np.asarray(inputs["L0_ffn_w_down"])),
        "sg": np.asarray(inputs["L0_ffn_s_gate"]),
        "su": np.asarray(inputs["L0_ffn_s_up"]),
        "sd": np.asarray(inputs["L0_ffn_s_down"]),
        "W1": np.asarray(inputs["L1_attn_W"]),
        "dW1": np.asarray(inputs["L1_attn_delta_W"]),
        "g1": pre_gc(np.asarray(inputs["L1_ffn_w_gate"])),
        "u1": pre_gc(np.asarray(inputs["L1_ffn_w_up"])),
        "d1": pre_dn(np.asarray(inputs["L1_ffn_w_down"])),
    }
    return base


def kernel(**inputs):
    x = np.asarray(inputs["L0_attn_x"])
    xr = x.reshape(B, NBK, 128, D)
    nc = build_nc()
    base = host_base(inputs)
    in_maps = []
    for c in range(8):
        b, p = divmod(c, 2)
        own = xr[b, p::2].reshape(TOK, D)
        m = dict(base)
        m["xoT"] = np.ascontiguousarray(own.T)
        m["xsT"] = np.ascontiguousarray(x[b].T)
        m["pofs"] = np.asarray([128.0 * p], dtype=np.float32)
        in_maps.append(m)
    res = run_bass_kernel_spmd(nc, in_maps, core_ids=list(range(8)))
    out = np.empty((B, NBK, 128, D), dtype=np.float32)
    for c in range(8):
        b, p = divmod(c, 2)
        out[b, p::2] = res.results[c]["yT"].T.reshape(OWN, 128, D)
    return out.reshape(B, T, D)


# version 06 (final)
# speedup vs baseline: 1.1195x; 1.1195x over previous
import numpy as np
from contextlib import ExitStack

import concourse.bass as bass
import concourse.mybir as mybir
from concourse.bass_utils import run_bass_kernel_spmd

F32 = mybir.dt.float32
BF16 = mybir.dt.bfloat16
I8 = mybir.dt.int8
AF = mybir.ActivationFunctionType
AL = mybir.AluOpType

B, T, D = 4, 2048, 2048
HD, NH, INTER = 512, 3, 8192
FC = D // 128
NBK = T // 128
OWN = 8
TOK = OWN * 128
GCN = INTER // 512
EPS = 1e-5
N_BLOCKS = 4


def build_nc():
    nc = bass.Bass("TRN2", target_bir_lowering=False)
    ap = lambda t: t.ap()

    xoT = ap(nc.dram_tensor("xoT", [D, TOK], F32, kind="ExternalInput"))
    xsT = ap(nc.dram_tensor("xsT", [D, T], F32, kind="ExternalInput"))
    pofs = ap(nc.dram_tensor("pofs", [1], F32, kind="ExternalInput"))
    Wq = ap(nc.dram_tensor("Wq", [128, FC * HD], F32, kind="ExternalInput"))
    Wk = ap(nc.dram_tensor("Wk", [NH, 128, FC * HD], F32, kind="ExternalInput"))
    Wv = ap(nc.dram_tensor("Wv", [NH, 128, FC * HD], F32, kind="ExternalInput"))
    Wo = ap(nc.dram_tensor("Wo", [128, 4 * D], F32, kind="ExternalInput"))
    Wg = ap(nc.dram_tensor("Wg", [128, FC * NH], F32, kind="ExternalInput"))
    dlog = ap(nc.dram_tensor("dlog", [NH], F32, kind="ExternalInput"))
    osc = ap(nc.dram_tensor("osc", [1], F32, kind="ExternalInput"))
    wg8 = ap(nc.dram_tensor("wg8", [GCN, 128, 8192], I8, kind="ExternalInput"))
    wu8 = ap(nc.dram_tensor("wu8", [GCN, 128, 8192], I8, kind="ExternalInput"))
    wd8 = ap(nc.dram_tensor("wd8", [2, GCN, 128, 4096], I8, kind="ExternalInput"))
    sg = ap(nc.dram_tensor("sg", [INTER], F32, kind="ExternalInput"))
    su = ap(nc.dram_tensor("su", [INTER], F32, kind="ExternalInput"))
    sd = ap(nc.dram_tensor("sd", [D], F32, kind="ExternalInput"))
    W1 = ap(nc.dram_tensor("W1", [D, D], F32, kind="ExternalInput"))
    dW1 = ap(nc.dram_tensor("dW1", [D, D], F32, kind="ExternalInput"))
    g1 = ap(nc.dram_tensor("g1", [GCN, 128, 8192], F32, kind="ExternalInput"))
    u1 = ap(nc.dram_tensor("u1", [GCN, 128, 8192], F32, kind="ExternalInput"))
    d1 = ap(nc.dram_tensor("d1", [2, GCN, 128, 4096], F32, kind="ExternalInput"))
    yT = ap(nc.dram_tensor("yT", [D, TOK], F32, kind="ExternalOutput"))

    es = ExitStack()
    with es:
        sb = lambda name, shape, dt: es.enter_context(nc.sbuf_tensor(name, shape, dt))
        XR = sb("XR", [128, FC * 1024], F32)
        AR = sb("AR", [128, 40960], BF16)
        WG = sb("WG", [128, 16384], BF16)
        WD = sb("WD", [128, 8192], BF16)
        MS = sb("MS", [128, 16], F32)
        SCL = sb("SCL", [128, 144], F32)
        SRW = sb("SRW", [1, 1024], F32)
        GR = sb("GR", [3, 1024], F32)
        WGT = sb("WGT", [128, 48], BF16)
        ONER = sb("ONER", [1, 128], F32)
        ONE3 = sb("ONE3", [3, 1], F32)
        ONEC = sb("ONEC", [128, 1], BF16)
        E3 = sb("E3", [3, 384], F32)
        IDT = sb("IDT", [128, 128], BF16)
        MI = sb("MI", [128, 128], F32)
        PS = es.enter_context(nc.psum_tensor("PS", [128, 4096], F32))

        names = ["sq", "gq", "tm", "vc", "sc", "sw", "gp",
                 "ga", "gu", "gw0", "gw1", "sqa", "sqb", "sda", "sdb"]
        SEM = {n: es.enter_context(nc.semaphore(n)) for n in names}
        C = {n: 0 for n in names}
        block = es.enter_context(nc.Block())

        def inc(inst, s, amt=1):
            C[s] += amt
            inst.then_inc(SEM[s], amt)
            return C[s]

        def w(eng, s, v):
            if v > 0:
                eng.wait_ge(SEM[s], v)

        MSc = lambda i: MS[:, i:i + 1]
        POFS, OSCC, LND0, EPSC = 0, 1, 2, 5

        def bank(i):
            return PS[:, 512 * i:512 * (i + 1)]

        def arf(c0, n):
            return AR[:, c0:c0 + 2 * n].bitcast(F32)

        def xrb(ic, t0, t1):
            return XR[:, ic * 1024 + t0 // 2: ic * 1024 + t1 // 2].bitcast(BF16)

        def setup_g(g):
            inc(g.iota(MI[:], [[-1, 128]], base=0, channel_multiplier=1,
                       allow_small_or_imprecise_dtypes=True), "gp")
            inc(g.memset(MSc(EPSC), EPS), "gp")
            inc(g.memset(ONER[:], 1.0), "gp")
            inc(g.memset(ONE3[:], 1.0), "gp")
            inc(g.memset(ONEC[:], 1.0), "gp")
            inc(g.iota(E3[:].rearrange("p (j m) -> p j m", m=128), [[-1, 3], [0, 128]],
                       base=0, channel_multiplier=1,
                       allow_small_or_imprecise_dtypes=True), "gp")
            inc(g.dma_start(out=MSc(POFS), in_=pofs.partition_broadcast(128)), "gq", 16)
            inc(g.dma_start(out=MSc(OSCC), in_=osc.partition_broadcast(128)), "gq", 16)
            for h in range(3):
                inc(g.dma_start(out=MSc(LND0 + h), in_=dlog[h:h + 1].partition_broadcast(128)), "gq", 16)
            inc(g.dma_start(out=SCL[:, 0:64].rearrange("p (g r) -> p g r", r=4),
                            in_=sg.rearrange("(g p r) -> p g r", p=128, r=4)), "gq", 16)
            inc(g.dma_start(out=SCL[:, 64:128].rearrange("p (g r) -> p g r", r=4),
                            in_=su.rearrange("(g p r) -> p g r", p=128, r=4)), "gq", 16)
            inc(g.dma_start(out=SCL[:, 128:144],
                            in_=sd.rearrange("(c p) -> p c", p=128),
                            allow_slow_non_contiguous=True), "sw", 16)
            inc(g.dma_start(out=WGT[:], in_=Wg[:, :]), "gq", 16)
        block.gpsimd(setup_g)
        GQ_SETUP = C["gq"]
        GP_SETUP = C["gp"]

        def setup_v(v):
            w(v, "gq", GQ_SETUP)
            w(v, "gp", GP_SETUP)
            inc(v.tensor_scalar(E3[:], E3[:], 0.0, None, op0=AL.is_equal), "vc")
            inc(v.tensor_scalar(IDT[:], MI[:], 0.0, None, op0=AL.is_equal), "vc")
        block.vector(setup_v)

        def setup_s(s):
            w(s, "gq", GQ_SETUP)
            w(s, "gp", GP_SETUP)
            for h in range(3):
                inc(s.activation(MSc(LND0 + h), MSc(LND0 + h), AF.Sigmoid), "sc")
        block.scalar(setup_s)
        SC_SIG = C["sc"]

        def setup_v2(v):
            w(v, "sc", SC_SIG)
            w(v, "sw", 16)
            for h in range(3):
                inc(v.tensor_scalar(MSc(LND0 + h), MSc(LND0 + h), 1e-38, None, op0=AL.max), "vc")
        block.vector(setup_v2)
        VC_CLAMP = C["vc"]

        def setup_s2(s):
            w(s, "vc", VC_CLAMP)
            for h in range(3):
                inc(s.activation(MSc(LND0 + h), MSc(LND0 + h), AF.Ln), "sc")
        block.scalar(setup_s2)
        SC_LND = C["sc"]

        def setup_t(t):
            w(t, "gp", GP_SETUP)
        block.tensor(setup_t)

        def emit_ln(ntok, dst, scr0, src=None, hsrc=None, hcol=0):
            ng = ntok // 512
            xb = [AR[:, scr0:scr0 + 512], AR[:, scr0 + 512:scr0 + 1024]]
            xq = [AR[:, scr0 + 1024:scr0 + 1536], AR[:, scr0 + 1536:scr0 + 2048]]
            stg = [arf(scr0 + 2048, 512), arf(scr0 + 3072, 512)]
            LNF1 = arf(scr0 + 4096, 512)
            LNF2 = arf(scr0 + 5120, 512)
            tm0, vc0, sc0 = C["tm"], C["vc"], C["sc"]

            for g in range(ng):
                t0 = g * 512
                tm_mms = {}
                vc_prep = {}
                sc_prep = {}
                sq_ld = {}
                for ic in range(FC):
                    bi = ic % 2
                    qs = "sqa" if bi == 0 else "sqb"
                    if hsrc is not None:
                        vw = vc_prep.get(ic - 2, C["vc"])
                        sw = sc_prep.get(ic - 2, C["sc"])

                        def ld(sy, ic=ic, bi=bi, t0=t0, vw=vw, sw=sw, qs=qs):
                            w(sy, "vc", vw)
                            w(sy, "sc", sw)
                            w(sy, "tm", tm0)
                            inc(sy.dma_start(out=stg[bi][:, 0:512],
                                             in_=hsrc[ic * 128:(ic + 1) * 128,
                                                      hcol + t0:hcol + t0 + 512]), qs, 16)
                        block.sync(ld)
                        sq_ld[ic] = C[qs]
                        srcap = stg[bi][:, 0:512]
                    else:
                        srcap = src(ic)[:, t0:t0 + 512]
                    tw = tm_mms.get(ic - 2, tm0)
                    qw = sq_ld.get(ic, 0)

                    def prep(v, srcap=srcap, bi=bi, tw=tw, qw=qw, qs=qs):
                        w(v, qs, qw)
                        w(v, "tm", tw)
                        w(v, "vc", C["vc"])
                        inc(v.tensor_copy(xb[bi], srcap), "vc")
                    block.vector(prep)
                    vc_prep[ic] = C["vc"]

                    def prepsq(s, srcap=srcap, bi=bi, tw=tw, qw=qw, qs=qs):
                        w(s, qs, qw)
                        w(s, "tm", tw)
                        w(s, "vc", vc0)
                        w(s, "sc", C["sc"])
                        inc(s.activation(xq[bi], srcap, AF.Square), "sc")
                    block.scalar(prepsq)
                    sc_prep[ic] = C["sc"]
                    vcn, scn = C["vc"], C["sc"]

                    def mms(t, ic=ic, bi=bi, vcn=vcn, scn=scn):
                        w(t, "vc", vcn)
                        w(t, "sc", scn)
                        inc(t.matmul(PS[0:1, 3072:3584], ONEC[:], xb[bi],
                                     start=(ic == 0), stop=(ic == FC - 1)), "tm")
                        inc(t.matmul(PS[0:1, 3584:4096], ONEC[:], xq[bi],
                                     start=(ic == 0), stop=(ic == FC - 1)), "tm")
                    block.tensor(mms)
                    tm_mms[ic] = C["tm"]
                tm_stats = C["tm"]

                def evac_rows(v):
                    w(v, "tm", tm_stats)
                    w(v, "vc", C["vc"])
                    inc(v.tensor_copy(SRW[0:1, 0:512], PS[0:1, 3072:3584]), "vc")
                    inc(v.tensor_copy(SRW[0:1, 512:1024], PS[0:1, 3584:4096]), "vc")
                block.vector(evac_rows)
                vc_rows = C["vc"]

                def bcasts(t):
                    w(t, "vc", vc_rows)
                    inc(t.matmul(bank(0), ONER[:], SRW[0:1, 0:512], start=True, stop=True), "tm")
                    inc(t.matmul(bank(1), ONER[:], SRW[0:1, 512:1024], start=True, stop=True), "tm")
                block.tensor(bcasts)
                tm_bc = C["tm"]

                def derive_v(v):
                    w(v, "tm", tm_bc)
                    w(v, "vc", C["vc"])
                    inc(v.tensor_scalar(LNF1, bank(0), -1.0 / D, None, op0=AL.mult), "vc")
                block.vector(derive_v)

                def derive_s(s):
                    w(s, "tm", tm_bc)
                    w(s, "vc", C["vc"] - 1)
                    w(s, "sc", C["sc"])
                    inc(s.activation(LNF2, bank(0), AF.Square, scale=1.0 / D), "sc")
                block.scalar(derive_s)
                sc_d = C["sc"]

                def derive_v2(v):
                    w(v, "sc", sc_d)
                    w(v, "vc", C["vc"])
                    inc(v.scalar_tensor_tensor(bank(3), bank(1), 1.0 / D, LNF2,
                                               op0=AL.mult, op1=AL.subtract), "vc")
                block.vector(derive_v2)
                vc_var = C["vc"]

                def derive_s2(s):
                    w(s, "vc", vc_var)
                    w(s, "sc", C["sc"])
                    inc(s.activation(bank(3), bank(3), AF.Sqrt, bias=MSc(EPSC)), "sc")
                block.scalar(derive_s2)
                sc_sd = C["sc"]

                def derive_v3(v):
                    w(v, "sc", sc_sd)
                    w(v, "vc", C["vc"])
                    inc(v.reciprocal(LNF2, bank(3)), "vc")
                block.vector(derive_v3)
                vc_inv = C["vc"]

                scA = C["sc"]
                vc_nrm = {}
                for ic in range(FC):
                    bi = ic % 2
                    qs = "sqa" if bi == 0 else "sqb"
                    if hsrc is not None:
                        vw = vc_nrm.get(ic - 2, vc_inv)

                        def ld2(sy, ic=ic, bi=bi, t0=t0, vw=vw, qs=qs):
                            w(sy, "vc", vw)
                            w(sy, "sc", scA)
                            inc(sy.dma_start(out=stg[bi][:, 0:512],
                                             in_=hsrc[ic * 128:(ic + 1) * 128,
                                                      hcol + t0:hcol + t0 + 512]), qs, 16)
                        block.sync(ld2)
                        sq2 = C[qs]
                        srcap = stg[bi][:, 0:512]
                    else:
                        sq2 = 0
                        srcap = src(ic)[:, t0:t0 + 512]

                    def nrm(v, srcap=srcap, ic=ic, g=g, sq2=sq2, qs=qs):
                        w(v, qs, sq2)
                        w(v, "tm", tm0)
                        w(v, "vc", C["vc"])
                        n1 = inc(v.tensor_tensor(bank(5), srcap, LNF1, op=AL.add), "vc")
                        v.wait_ge(SEM["vc"], n1)
                        inc(v.tensor_tensor(dst(ic, g), bank(5), LNF2, op=AL.mult), "vc")
                    block.vector(nrm)
                    vc_nrm[ic] = C["vc"]

        def emit_l0attn():
            A_XNH = 0
            A_Q = 8192
            A_K = 12288
            A_V = 20480
            A_MK = 28672
            A_MSF = 3072
            A_SC = 0
            A_GBC = 1024
            A_BL = 32768

            def ldwq(gp):
                inc(gp.dma_start(out=WG[:, 0:8192], in_=Wq[:, :]), "ga", 16)
            block.gpsimd(ldwq)
            gq_wq = C["ga"]

            for g in range(2):
                emit_ln(512,
                        lambda ic, gg, g=g: AR[:, A_XNH + ic * 512:A_XNH + ic * 512 + 512],
                        A_K, hsrc=xoT, hcol=g * 512)
                vc_xnh = C["vc"]

                for dc in range(4):
                    bq = dc % 2
                    scw = C["sc"] - 1 if dc >= 2 else C["sc"]

                    def qmm(t, dc=dc, bq=bq, vc_xnh=vc_xnh, scw=scw):
                        w(t, "vc", vc_xnh)
                        w(t, "ga", gq_wq)
                        w(t, "sc", scw)
                        for ic in range(FC):
                            inc(t.matmul(bank(bq), WG[:, ic * 512 + dc * 128:ic * 512 + dc * 128 + 128],
                                         AR[:, A_XNH + ic * 512:A_XNH + ic * 512 + 512],
                                         start=(ic == 0), stop=(ic == FC - 1)), "tm")
                    block.tensor(qmm)
                    tmq = C["tm"]

                    def qev(s, dc=dc, bq=bq, g=g, tmq=tmq):
                        w(s, "tm", tmq)
                        w(s, "sc", C["sc"])
                        inc(s.activation(AR[:, A_Q + dc * 1024 + g * 512:A_Q + dc * 1024 + g * 512 + 512],
                                         bank(bq), AF.Copy), "sc")
                    block.scalar(qev)

                vcw_g = C["vc"]
                scw_g = C["sc"]

                def gmm(t, vcw_g=vcw_g):
                    w(t, "vc", vcw_g)
                    w(t, "sc", scw_g)
                    for ic in range(FC):
                        inc(t.matmul(PS[0:3, 1024:1536], WGT[:, ic * 3:ic * 3 + 3],
                                     AR[:, A_XNH + ic * 512:A_XNH + ic * 512 + 512],
                                     start=(ic == 0), stop=(ic == FC - 1)), "tm")
                block.tensor(gmm)
                tmg = C["tm"]

                def gev(v, g=g, tmg=tmg):
                    w(v, "tm", tmg)
                    w(v, "vc", C["vc"])
                    inc(v.tensor_copy(GR[0:3, g * 512:g * 512 + 512], PS[0:3, 1024:1536]), "vc")
                block.vector(gev)

            vc_g = C["vc"]

            def gsm_t1(t):
                w(t, "vc", vc_g)
                inc(t.matmul(PS[0:1, 3072:3584], ONE3[:], GR[0:3, 0:512], start=True, stop=True), "tm")
                inc(t.matmul(PS[0:1, 3584:4096], ONE3[:], GR[0:3, 512:1024], start=True, stop=True), "tm")
            block.tensor(gsm_t1)
            tm_gs = C["tm"]

            def gsm_v1(v):
                w(v, "tm", tm_gs)
                w(v, "vc", C["vc"])
                inc(v.tensor_scalar(PS[0:1, 3072:3584], PS[0:1, 3072:3584], -1.0 / 3.0, None, op0=AL.mult), "vc")
                inc(v.tensor_scalar(PS[0:1, 3584:4096], PS[0:1, 3584:4096], -1.0 / 3.0, None, op0=AL.mult), "vc")
                v.wait_ge(SEM["vc"], C["vc"])
                inc(v.tensor_copy(SRW[0:1, 0:512], PS[0:1, 3072:3584]), "vc")
                inc(v.tensor_copy(SRW[0:1, 512:1024], PS[0:1, 3584:4096]), "vc")
            block.vector(gsm_v1)
            vc_gs = C["vc"]

            def gsm_t2(t):
                w(t, "vc", vc_gs)
                inc(t.matmul(PS[0:3, 1024:1536], ONER[0:1, 0:3], SRW[0:1, 0:512], start=True, stop=True), "tm")
                inc(t.matmul(PS[0:3, 1536:2048], ONER[0:1, 0:3], SRW[0:1, 512:1024], start=True, stop=True), "tm")
            block.tensor(gsm_t2)
            tm_gs2 = C["tm"]

            def gsm_v2(v):
                w(v, "tm", tm_gs2)
                w(v, "vc", C["vc"])
                inc(v.tensor_tensor(GR[0:3, :], GR[0:3, :], PS[0:3, 1024:2048], op=AL.add), "vc")
            block.vector(gsm_v2)
            vc_gs2 = C["vc"]

            def gsm_s(s):
                w(s, "vc", vc_gs2)
                w(s, "sc", C["sc"])
                inc(s.activation(GR[0:3, :], GR[0:3, :], AF.Exp), "sc")
            block.scalar(gsm_s)
            sc_gs = C["sc"]

            def gsm_t3(t):
                w(t, "sc", sc_gs)
                inc(t.matmul(PS[0:1, 3072:3584], ONE3[:], GR[0:3, 0:512], start=True, stop=True), "tm")
                inc(t.matmul(PS[0:1, 3584:4096], ONE3[:], GR[0:3, 512:1024], start=True, stop=True), "tm")
            block.tensor(gsm_t3)
            tm_gs3 = C["tm"]

            def gsm_v3(v):
                w(v, "tm", tm_gs3)
                w(v, "vc", C["vc"])
                inc(v.reciprocal(SRW[0:1, 0:512], PS[0:1, 3072:3584]), "vc")
                inc(v.reciprocal(SRW[0:1, 512:1024], PS[0:1, 3584:4096]), "vc")
            block.vector(gsm_v3)
            vc_gs3 = C["vc"]

            def gsm_t4(t):
                w(t, "vc", vc_gs3)
                inc(t.matmul(PS[0:3, 1024:1536], ONER[0:1, 0:3], SRW[0:1, 0:512], start=True, stop=True), "tm")
                inc(t.matmul(PS[0:3, 1536:2048], ONER[0:1, 0:3], SRW[0:1, 512:1024], start=True, stop=True), "tm")
            block.tensor(gsm_t4)
            tm_gs4 = C["tm"]

            def gsm_v4(v):
                w(v, "tm", tm_gs4)
                w(v, "vc", C["vc"])
                inc(v.tensor_tensor(GR[0:3, :], GR[0:3, :], PS[0:3, 1024:2048], op=AL.mult), "vc")
            block.vector(gsm_v4)

            emit_ln(T, lambda ic, g: xrb(ic, g * 512, g * 512 + 512),
                    A_V, hsrc=xsT, hcol=0)
            VC_XNSEQ = C["vc"]

            MSF = arf(A_MSF, 2048)
            tm_msf = C["tm"]

            def msf_iota(g):
                w(g, "tm", tm_msf)
                inc(g.iota(MSF.rearrange("p (j m) -> p j m", m=128),
                           [[128, 16], [-1, 128]], base=0, channel_multiplier=1,
                           allow_small_or_imprecise_dtypes=True), "gp")
            block.gpsimd(msf_iota)
            gp_msf = C["gp"]

            def msf_adj(v):
                w(v, "gp", gp_msf)
                w(v, "vc", C["vc"])
                inc(v.tensor_scalar(MSF, MSF, 1.0, MSc(POFS),
                                    op0=AL.subtract, op1=AL.subtract), "vc")
            block.vector(msf_adj)

            tm_head0 = C["tm"]
            tm_kv_done = [0, 0, 0]
            for h in range(3):
                tkw = tm_kv_done[h - 1] if h >= 1 else tm_head0

                def ldk(gp, h=h, tkw=tkw):
                    w(gp, "tm", tkw)
                    inc(gp.dma_start(out=WG[:, 0:8192], in_=Wk[h]), "ga", 16)
                block.gpsimd(ldk)
                gq_k = C["ga"]

                def ldv(gp, h=h, tkw=tkw):
                    w(gp, "tm", tkw)
                    inc(gp.dma_start(out=WG[:, 8192:16384], in_=Wv[h]), "gu", 16)
                block.gpsimd(ldv)
                gq_v = C["gu"]

                for dc in range(4):
                    for sg_ in range(4):
                        idx = dc * 4 + sg_
                        bq = idx % 2
                        scw = C["sc"] - 1 if idx >= 2 else C["sc"]

                        def kmm(t, dc=dc, sg_=sg_, bq=bq, idx=idx, scw=scw):
                            if idx == 0:
                                w(t, "vc", VC_XNSEQ)
                                w(t, "ga", gq_k)
                            w(t, "sc", scw)
                            for ic in range(FC):
                                inc(t.matmul(bank(bq),
                                             WG[:, ic * 512 + dc * 128:ic * 512 + dc * 128 + 128],
                                             xrb(ic, sg_ * 512, sg_ * 512 + 512),
                                             start=(ic == 0), stop=(ic == FC - 1)), "tm")
                        block.tensor(kmm)
                        tmk = C["tm"]

                        def kev(s, dc=dc, sg_=sg_, bq=bq, tmk=tmk):
                            w(s, "tm", tmk)
                            w(s, "sc", C["sc"])
                            inc(s.activation(AR[:, A_K + dc * 2048 + sg_ * 512:
                                                A_K + dc * 2048 + sg_ * 512 + 512],
                                             bank(bq), AF.Copy), "sc")
                        block.scalar(kev)
                for j in range(NBK):
                    bq = j % 2
                    scw = C["sc"] - 1 if j >= 2 else C["sc"]

                    def vmm(t, j=j, bq=bq, scw=scw):
                        if j == 0:
                            w(t, "gu", gq_v)
                        w(t, "sc", scw)
                        for ic in range(FC):
                            inc(t.matmul(bank(bq),
                                         xrb(ic, j * 128, j * 128 + 128),
                                         WG[:, 8192 + ic * 512:8192 + ic * 512 + 512],
                                         start=(ic == 0), stop=(ic == FC - 1)), "tm")
                    block.tensor(vmm)
                    tmv = C["tm"]

                    def vev(s, j=j, bq=bq, tmv=tmv):
                        w(s, "tm", tmv)
                        w(s, "sc", C["sc"])
                        inc(s.activation(AR[:, A_V + j * 512:A_V + j * 512 + 512],
                                         bank(bq), AF.Copy), "sc")
                    block.scalar(vev)
                tm_kv_done[h] = C["tm"]

                MKall = AR[:, A_MK:A_MK + 2048]
                PSF = PS[:, 1024:3072]

                def mkb1(v, h=h):
                    w(v, "sc", SC_LND)
                    w(v, "tm", C["tm"])
                    w(v, "vc", C["vc"])
                    inc(v.tensor_scalar(PSF, MSF, MSc(LND0 + h), 0.0,
                                        op0=AL.mult, op1=AL.min), "vc")
                block.vector(mkb1)
                vcb1 = C["vc"]

                def mkb2(s, vcb1=vcb1):
                    w(s, "vc", vcb1)
                    w(s, "sc", C["sc"])
                    inc(s.activation(PSF, PSF, AF.Exp), "sc")
                block.scalar(mkb2)
                scb2 = C["sc"]

                def mkb3(v):
                    w(v, "vc", C["vc"])
                    inc(v.tensor_scalar(MKall, MSF, 0.0, None, op0=AL.is_ge), "vc")
                block.vector(mkb3)

                def mkb4(v, scb2=scb2):
                    w(v, "sc", scb2)
                    w(v, "vc", C["vc"])
                    inc(v.tensor_tensor(MKall, MKall, PSF, op=AL.mult), "vc")
                block.vector(mkb4)
                vc_masks = C["vc"]

                def gbc(t, h=h, vc_masks=vc_masks):
                    w(t, "vc", vc_masks)
                    inc(t.matmul(bank(6), E3[:, h * 128:h * 128 + 128], GR[0:3, 0:512],
                                 start=True, stop=True), "tm")
                    inc(t.matmul(bank(7), E3[:, h * 128:h * 128 + 128], GR[0:3, 512:1024],
                                 start=True, stop=True), "tm")
                block.tensor(gbc)
                tm_gbc = C["tm"]

                def gbcev(v, tm_gbc=tm_gbc):
                    w(v, "tm", tm_gbc)
                    w(v, "vc", C["vc"])
                    gb = arf(A_GBC, 1024)
                    inc(v.tensor_copy(gb[:, 0:512], bank(6)), "vc")
                    inc(v.tensor_copy(gb[:, 512:1024], bank(7)), "vc")
                block.vector(gbcev)
                vc_gbc = C["vc"]

                vc_rev = {}
                vc_mmd = {}
                for j in range(NBK):
                    nj = j // 2 + 1
                    Nj = nj * 128
                    npieces = (Nj + 511) // 512
                    sb0 = 0 if j % 2 == 0 else 3072
                    vcw = vc_mmd.get(j - 2, vc_gbc)

                    def scmm(t, j=j, Nj=Nj, npieces=npieces, vcw=vcw, sb0=sb0):
                        w(t, "vc", vcw)
                        for pc in range(npieces):
                            n0 = pc * 512
                            n1 = min(Nj, n0 + 512)
                            for dc in range(4):
                                inc(t.matmul(PS[:, sb0 + n0:sb0 + n1],
                                             AR[:, A_K + dc * 2048 + j * 128:A_K + dc * 2048 + j * 128 + 128],
                                             AR[:, A_Q + dc * 1024 + n0:A_Q + dc * 1024 + n1],
                                             start=(dc == 0), stop=(dc == 3)), "tm")
                    block.tensor(scmm)
                    tm_sc = C["tm"]

                    def mmul(v, j=j, nj=nj, tm_sc=tm_sc, sb0=sb0):
                        w(v, "tm", tm_sc)
                        w(v, "vc", C["vc"])
                        for bi in range(nj):
                            dp = j - 2 * bi
                            inc(v.tensor_tensor(AR[:, A_SC + bi * 128:A_SC + bi * 128 + 128],
                                                PS[:, sb0 + bi * 128:sb0 + bi * 128 + 128],
                                                AR[:, A_MK + dp * 128:A_MK + dp * 128 + 128],
                                                op=AL.mult), "vc")
                    block.vector(mmul)
                    vc_mm = C["vc"]
                    vc_mmd[j] = vc_mm

                    for dc in range(4):
                        rb = 2 + 2 * (dc % 2)
                        ridx = j * 4 + dc
                        rvw = vc_rev.get(ridx - 2, 0)

                        def rmm(t, j=j, Nj=Nj, dc=dc, rb=rb, vc_mm=vc_mm, rvw=rvw):
                            w(t, "vc", max(vc_mm, rvw))
                            for pc in range((Nj + 511) // 512):
                                n0 = pc * 512
                                n1 = min(Nj, n0 + 512)
                                inc(t.matmul(PS[:, rb * 512 + n0:rb * 512 + n1],
                                             AR[:, A_V + j * 512 + dc * 128:A_V + j * 512 + dc * 128 + 128],
                                             AR[:, A_SC + n0:A_SC + n1],
                                             start=True, stop=True), "tm")
                        block.tensor(rmm)
                        tm_r = C["tm"]

                        def rev(v, j=j, Nj=Nj, dc=dc, rb=rb, h=h, tm_r=tm_r):
                            w(v, "tm", tm_r)
                            w(v, "vc", C["vc"])
                            blv = arf(A_BL + dc * 2048, 1024)
                            gb = arf(A_GBC, 1024)
                            r1 = inc(v.tensor_tensor(PS[:, rb * 512:rb * 512 + Nj],
                                                     PS[:, rb * 512:rb * 512 + Nj],
                                                     gb[:, 0:Nj], op=AL.mult), "vc")
                            v.wait_ge(SEM["vc"], r1)
                            if h == 0 and j % 2 == 0:
                                new0 = Nj - 128
                                if new0 > 0:
                                    inc(v.tensor_tensor(blv[:, 0:new0], blv[:, 0:new0],
                                                        PS[:, rb * 512:rb * 512 + new0], op=AL.add), "vc")
                                inc(v.tensor_copy(blv[:, new0:Nj], PS[:, rb * 512 + new0:rb * 512 + Nj]), "vc")
                            else:
                                inc(v.tensor_tensor(blv[:, 0:Nj], blv[:, 0:Nj],
                                                    PS[:, rb * 512:rb * 512 + Nj], op=AL.add), "vc")
                        block.vector(rev)
                        vc_rev[ridx] = C["vc"]

            vc_ret = C["vc"]

            def blb(v):
                w(v, "vc", vc_ret)
                for dc in range(4):
                    inc(v.tensor_copy(AR[:, A_Q + dc * 1024:A_Q + dc * 1024 + 1024],
                                      arf(A_BL + dc * 2048, 1024)), "vc")
            block.vector(blb)
            vc_blb = C["vc"]

            def ldwo(gp):
                w(gp, "tm", tm_kv_done[2])
                inc(gp.dma_start(out=WD[:], in_=Wo[:, :]), "gu", 16)
            block.gpsimd(ldwo)
            gq_wo = C["gu"]

            xobuf = [arf(A_K, 512), arf(A_K + 1024, 512)]
            vc_woev = {}
            for oc in range(FC):
                for th in range(2):
                    idx = oc * 2 + th
                    bi = idx % 2
                    lvw = vc_woev.get(idx - 2, vc_blb)

                    xqs = "sqa" if bi == 0 else "sqb"

                    def ldxo(sy, oc=oc, th=th, bi=bi, lvw=lvw, xqs=xqs):
                        w(sy, "vc", lvw)
                        inc(sy.dma_start(out=xobuf[bi][:, 0:512],
                                         in_=xoT[oc * 128:(oc + 1) * 128, th * 512:th * 512 + 512]), xqs, 16)
                    block.sync(ldxo)
                    sq_xo = C[xqs]
                    bq = idx % 2
                    wvw = vc_woev.get(idx - 2, vc_blb)

                    def womm(t, oc=oc, th=th, bq=bq, idx=idx, wvw=wvw):
                        if idx == 0:
                            w(t, "vc", vc_blb)
                            w(t, "gu", gq_wo)
                            w(t, "sc", C["sc"])
                        w(t, "vc", wvw)
                        for dc in range(4):
                            inc(t.matmul(bank(bq),
                                         WD[:, dc * 2048 + oc * 128:dc * 2048 + oc * 128 + 128],
                                         AR[:, A_Q + dc * 1024 + th * 512:A_Q + dc * 1024 + th * 512 + 512],
                                         start=(dc == 0), stop=(dc == 3)), "tm")
                    block.tensor(womm)
                    tm_wo = C["tm"]

                    def woev(v, oc=oc, th=th, bq=bq, bi=bi, tm_wo=tm_wo, sq_xo=sq_xo):
                        w(v, "tm", tm_wo)
                        w(v, xqs, sq_xo)
                        w(v, "vc", C["vc"])
                        inc(v.scalar_tensor_tensor(
                            XR[:, oc * 1024 + th * 512:oc * 1024 + th * 512 + 512],
                            bank(bq), MSc(OSCC), xobuf[bi][:, 0:512],
                            op0=AL.mult, op1=AL.add), "vc")
                    block.vector(woev)
                    vc_woev[idx] = C["vc"]

        def emit_ffn(layer, wg_free_tm, wd_free_tm):
            A_XN = 0
            A_H = 8192
            A_LNS = A_H
            wq_, uq_, dq_ = (wg8, wu8, wd8) if layer == 0 else (g1, u1, d1)
            st = {"tm_gate": wg_free_tm, "tm_up": wg_free_tm,
                  "tm_down": [wd_free_tm, wd_free_tm]}

            for tt in range(2):
                emit_ln(512,
                        lambda ic, g: AR[:, A_XN + ic * 512:A_XN + ic * 512 + 512],
                        A_LNS,
                        src=lambda ic, tt=tt: XR[:, ic * 1024 + tt * 512:ic * 1024 + tt * 512 + 512])
                vc_xn = C["vc"]

                vc_fuse_prev = 0
                for gc in range(GCN):
                    tgw = st["tm_gate"]

                    def ldg(gp, gc=gc, tgw=tgw):
                        w(gp, "tm", tgw)
                        inc(gp.dma_start(out=WG[:, 0:8192], in_=wq_[gc]), "ga", 16)
                    block.gpsimd(ldg)
                    gq_g = C["ga"]
                    tuw = st["tm_up"]

                    def ldu(gp, gc=gc, tuw=tuw):
                        w(gp, "tm", tuw)
                        inc(gp.dma_start(out=WG[:, 8192:16384], in_=uq_[gc]), "gu", 16)
                    block.gpsimd(ldu)
                    gq_u = C["gu"]

                    def gatemm(t, gc=gc, gq_g=gq_g, vcp=vc_fuse_prev, vcx=vc_xn):
                        w(t, "ga", gq_g)
                        w(t, "vc", max(vcp, vcx))
                        for r in range(4):
                            for ic in range(FC):
                                inc(t.matmul(PS[:, r * 512:(r + 1) * 512],
                                             WG[:, ic * 512 + r:ic * 512 + 512:4],
                                             AR[:, A_XN + ic * 512:A_XN + ic * 512 + 512],
                                             start=(ic == 0), stop=(ic == FC - 1)), "tm")
                    block.tensor(gatemm)
                    st["tm_gate"] = C["tm"]
                    tm_gate = C["tm"]

                    def upmm(t, gc=gc, gq_u=gq_u, vcp=vc_fuse_prev):
                        w(t, "gu", gq_u)
                        w(t, "vc", vcp)
                        for r in range(4):
                            for ic in range(FC):
                                inc(t.matmul(PS[:, 2048 + r * 512:2048 + (r + 1) * 512],
                                             WG[:, 8192 + ic * 512 + r:8192 + ic * 512 + 512:4],
                                             AR[:, A_XN + ic * 512:A_XN + ic * 512 + 512],
                                             start=(ic == 0), stop=(ic == FC - 1)), "tm")
                    block.tensor(upmm)
                    st["tm_up"] = C["tm"]
                    tm_up = C["tm"]

                    if layer == 0:
                        def siluev(s, gc=gc, tm_up=tm_up):
                            w(s, "tm", tm_up)
                            w(s, "sc", C["sc"])
                            for r in range(4):
                                inc(s.activation(
                                    AR[:, A_H + gc * 2048 + r * 512:A_H + gc * 2048 + (r + 1) * 512],
                                    PS[:, r * 512:(r + 1) * 512], AF.Silu,
                                    scale=SCL[:, gc * 4 + r:gc * 4 + r + 1]), "sc")
                        block.scalar(siluev)
                        sc_silu = C["sc"]

                        def fuse(v, gc=gc, sc_silu=sc_silu):
                            w(v, "sc", sc_silu)
                            w(v, "vc", C["vc"])
                            for r in range(4):
                                hs = AR[:, A_H + gc * 2048 + r * 512:A_H + gc * 2048 + (r + 1) * 512]
                                inc(v.scalar_tensor_tensor(hs,
                                                           PS[:, 2048 + r * 512:2048 + (r + 1) * 512],
                                                           SCL[:, 64 + gc * 4 + r:64 + gc * 4 + r + 1],
                                                           hs, op0=AL.mult, op1=AL.mult), "vc")
                        block.vector(fuse)
                    else:
                        HG = WD[:, 0:1024].bitcast(F32)
                        keep = WD[:, 0:512]
                        ABS = lambda r: WD[:, 1024 + r * 1024:1024 + (r + 1) * 1024].bitcast(F32)
                        T1 = WD[:, 5120:6144].bitcast(F32)
                        T2 = WD[:, 6144:7168].bitcast(F32)
                        T3 = WD[:, 7168:8192].bitcast(F32)
                        for r in range(4):
                            svw = C["vc"]

                            def silur(s, r=r, tm_up=tm_up, svw=svw):
                                w(s, "tm", tm_up)
                                w(s, "vc", svw)
                                w(s, "sc", C["sc"])
                                inc(s.activation(HG, PS[:, r * 512:(r + 1) * 512], AF.Silu), "sc")
                            block.scalar(silur)
                            sc_r = C["sc"]

                            def hr(v, r=r, sc_r=sc_r):
                                w(v, "sc", sc_r)
                                w(v, "vc", C["vc"])
                                inc(v.tensor_tensor(PS[:, r * 512:(r + 1) * 512], HG,
                                                    PS[:, 2048 + r * 512:2048 + (r + 1) * 512],
                                                    op=AL.mult), "vc")
                            block.vector(hr)
                        vc_h = C["vc"]

                        def absf(s, vc_h=vc_h):
                            w(s, "vc", vc_h)
                            w(s, "sc", C["sc"])
                            for r in range(4):
                                inc(s.activation(ABS(r), PS[:, r * 512:(r + 1) * 512], AF.Abs), "sc")
                        block.scalar(absf)
                        sc_abs = C["sc"]

                        T4 = PS[:, 2048:2560]
                        KP = [WD[:, 0:512], WD[:, 512:1024],
                              WD[:, 6144:6656], WD[:, 6656:7168]]

                        def mask(v, gc=gc, sc_abs=sc_abs):
                            w(v, "sc", sc_abs)
                            w(v, "vc", C["vc"])
                            inc(v.tensor_tensor(T1, ABS(0), ABS(1), op=AL.max), "vc")
                            inc(v.tensor_tensor(T2, ABS(2), ABS(3), op=AL.max), "vc")
                            inc(v.tensor_tensor(T3, ABS(0), ABS(1), op=AL.min), "vc")
                            inc(v.tensor_tensor(T4, ABS(2), ABS(3), op=AL.min), "vc")
                            v.wait_ge(SEM["vc"], C["vc"])
                            inc(v.tensor_tensor(T1, T1, T2, op=AL.min), "vc")
                            inc(v.tensor_tensor(T3, T3, T4, op=AL.max), "vc")
                            v.wait_ge(SEM["vc"], C["vc"])
                            inc(v.tensor_tensor(T1, T1, T3, op=AL.max), "vc")
                            v.wait_ge(SEM["vc"], C["vc"])
                            for r in range(4):
                                inc(v.tensor_tensor(KP[r], ABS(r), T1, op=AL.is_ge), "vc")
                            v.wait_ge(SEM["vc"], C["vc"])
                            for r in range(4):
                                inc(v.tensor_tensor(
                                    AR[:, A_H + gc * 2048 + r * 512:A_H + gc * 2048 + (r + 1) * 512],
                                    PS[:, r * 512:(r + 1) * 512], KP[r], op=AL.mult), "vc")
                        block.vector(mask)
                    vc_fuse_prev = C["vc"]
                vc_p1 = C["vc"]

                for dmh in range(2):
                    for gc in range(GCN):
                        wbuf = (gc % 2) * 4096
                        tdw = st["tm_down"][gc % 2]
                        mvw = vc_p1 if (layer == 1 and dmh == 0 and gc < 2) else 0

                        gws = "gw0" if gc % 2 == 0 else "gw1"

                        def ldd(gp, gc=gc, dmh=dmh, wbuf=wbuf, tdw=tdw, mvw=mvw, gws=gws):
                            w(gp, "tm", tdw)
                            w(gp, "vc", mvw)
                            inc(gp.dma_start(out=WD[:, wbuf:wbuf + 4096],
                                             in_=dq_[dmh, gc]), gws, 16)
                        block.gpsimd(ldd)
                        gq_d = C[gws]
                        dvw = C["vc"]

                        def downmm(t, gc=gc, wbuf=wbuf, gq_d=gq_d, dvw=dvw, gws=gws):
                            w(t, gws, gq_d)
                            if gc == 0:
                                w(t, "vc", max(vc_p1, dvw))
                                w(t, "sc", C["sc"])
                            for r in range(4):
                                for oc in range(8):
                                    inc(t.matmul(PS[:, oc * 512:(oc + 1) * 512],
                                                 WD[:, wbuf + r * 1024 + oc * 128:
                                                    wbuf + r * 1024 + oc * 128 + 128],
                                                 AR[:, A_H + gc * 2048 + r * 512:
                                                    A_H + gc * 2048 + (r + 1) * 512],
                                                 start=(gc == 0 and r == 0),
                                                 stop=(gc == GCN - 1 and r == 3)), "tm")
                        block.tensor(downmm)
                        st["tm_down"][gc % 2] = C["tm"]
                    tm_down = C["tm"]

                    def devac(v, dmh=dmh, tt=tt, layer=layer, tm_down=tm_down):
                        w(v, "tm", tm_down)
                        w(v, "vc", C["vc"])
                        for oc in range(8):
                            goc = dmh * 8 + oc
                            dst = XR[:, goc * 1024 + tt * 512:goc * 1024 + tt * 512 + 512]
                            if layer == 0:
                                inc(v.scalar_tensor_tensor(dst, PS[:, oc * 512:(oc + 1) * 512],
                                                           SCL[:, 128 + goc:128 + goc + 1], dst,
                                                           op0=AL.mult, op1=AL.add), "vc")
                            else:
                                inc(v.tensor_tensor(dst, PS[:, oc * 512:(oc + 1) * 512], dst,
                                                    op=AL.add), "vc")
                    block.vector(devac)

        def emit_l1attn(wg_free_vc, wd_free_tm):
            A_XN = 0
            A_LNS = 16384
            emit_ln(TOK,
                    lambda ic, g: AR[:, A_XN + ic * 1024 + g * 512:A_XN + ic * 1024 + g * 512 + 512],
                    A_LNS,
                    src=lambda ic: XR[:, ic * 1024:(ic + 1) * 1024])
            vc_xn3 = C["vc"]

            wgf = WG[:].bitcast(F32)
            vc_msum = [wg_free_vc, wg_free_vc]
            tm_tr = [wd_free_tm, wd_free_tm]
            vc_yev = {}
            for oc in range(FC):
                bsel = oc % 2
                lvw = vc_msum[bsel]

                wqs = "sqa" if bsel == 0 else "sqb"
                dqs = "sda" if bsel == 0 else "sdb"

                def ldw(sy, oc=oc, bsel=bsel, lvw=lvw, wqs=wqs):
                    w(sy, "vc", lvw)
                    inc(sy.dma_start(out=wgf[:, bsel * 2048:bsel * 2048 + 2048],
                                     in_=W1[oc * 128:(oc + 1) * 128, :]), wqs, 16)
                block.sync(ldw)
                sq_w = C[wqs]

                def ldw2(s, oc=oc, bsel=bsel, lvw=lvw, dqs=dqs):
                    w(s, "vc", lvw)
                    inc(s.dma_start(out=wgf[:, 4096 + bsel * 2048:4096 + bsel * 2048 + 2048],
                                    in_=dW1[oc * 128:(oc + 1) * 128, :]), dqs, 16)
                block.scalar(ldw2)
                sq_w2 = C[dqs]
                mtw = tm_tr[bsel]

                def msum(v, bsel=bsel, sq_w=sq_w, sq_w2=sq_w2, mtw=mtw, wqs=wqs, dqs=dqs):
                    w(v, wqs, sq_w)
                    w(v, dqs, sq_w2)
                    w(v, "tm", mtw)
                    w(v, "vc", C["vc"])
                    inc(v.tensor_tensor(WD[:, bsel * 2048:bsel * 2048 + 2048],
                                        wgf[:, bsel * 2048:bsel * 2048 + 2048],
                                        wgf[:, 4096 + bsel * 2048:4096 + bsel * 2048 + 2048],
                                        op=AL.add), "vc")
                block.vector(msum)
                vc_msum[bsel] = C["vc"]
                vc_ms = C["vc"]

                tdst = 4096 + bsel * 2048
                vc_trev = {}
                for grp in range(4):
                    tb = grp % 2
                    tvw = vc_trev.get(grp - 2, 0)

                    def trmm(t, grp=grp, tb=tb, bsel=bsel, vc_ms=vc_ms, tvw=tvw):
                        w(t, "vc", max(vc_ms, tvw))
                        for k in range(4):
                            ic = grp * 4 + k
                            inc(t.matmul(PS[:, tb * 512 + k * 128:tb * 512 + k * 128 + 128],
                                         WD[:, bsel * 2048 + ic * 128:bsel * 2048 + ic * 128 + 128],
                                         IDT[:], start=True, stop=True), "tm")
                    block.tensor(trmm)
                    tm_t = C["tm"]

                    def trev(v, grp=grp, tb=tb, tdst=tdst, tm_t=tm_t):
                        w(v, "tm", tm_t)
                        w(v, "vc", C["vc"])
                        inc(v.tensor_copy(WD[:, tdst + grp * 512:tdst + grp * 512 + 512],
                                          PS[:, tb * 512:tb * 512 + 512]), "vc")
                    block.vector(trev)
                    vc_trev[grp] = C["vc"]
                tm_tr[bsel] = C["tm"]
                vc_tr = C["vc"]

                for th in range(2):
                    idx = oc * 2 + th
                    yb = 2 + idx % 2
                    yvw = vc_yev.get(idx - 2, 0)

                    def ymm(t, oc=oc, th=th, yb=yb, tdst=tdst, vc_tr=vc_tr, yvw=yvw):
                        w(t, "vc", max(vc_tr, vc_xn3, yvw))
                        for ic in range(FC):
                            inc(t.matmul(PS[:, yb * 512:yb * 512 + 512],
                                         WD[:, tdst + ic * 128:tdst + ic * 128 + 128],
                                         AR[:, A_XN + ic * 1024 + th * 512:A_XN + ic * 1024 + th * 512 + 512],
                                         start=(ic == 0), stop=(ic == FC - 1)), "tm")
                    block.tensor(ymm)
                    tm_y = C["tm"]

                    def yev(v, oc=oc, th=th, yb=yb, tm_y=tm_y):
                        w(v, "tm", tm_y)
                        w(v, "vc", C["vc"])
                        dst = XR[:, oc * 1024 + th * 512:oc * 1024 + th * 512 + 512]
                        inc(v.tensor_tensor(dst, PS[:, yb * 512:yb * 512 + 512], dst,
                                            op=AL.add), "vc")
                    block.vector(yev)
                    vc_yev[idx] = C["vc"]

        if N_BLOCKS >= 1:
            emit_l0attn()
        if N_BLOCKS >= 2:
            emit_ffn(0, wg_free_tm=C["tm"], wd_free_tm=C["tm"])
        if N_BLOCKS >= 3:
            emit_l1attn(wg_free_vc=C["vc"], wd_free_tm=C["tm"])
        if N_BLOCKS >= 4:
            emit_ffn(1, wg_free_tm=C["tm"], wd_free_tm=C["tm"])

        vc_final = C["vc"]

        def store(sy):
            w(sy, "vc", vc_final)
            for ic in range(FC):
                inc(sy.dma_start(out=yT[ic * 128:(ic + 1) * 128, :],
                                 in_=XR[:, ic * 1024:(ic + 1) * 1024]), "sq", 16)
            sy.wait_ge(SEM["sq"], C["sq"])
        block.sync(store)

    return nc


def host_base(inputs):
    def pre_cm(wm):
        Cc, M = wm.shape[0] // 128, wm.shape[1]
        return np.ascontiguousarray(wm.reshape(Cc, 128, M).transpose(1, 0, 2).reshape(128, Cc * M))

    def pre_gc(wm):
        wr = wm.reshape(16, 128, 16, 512)
        return np.ascontiguousarray(wr.transpose(2, 1, 0, 3).reshape(16, 128, 8192))

    def pre_dn(wm):
        wr = wm.reshape(16, 128, 4, 2, 1024)
        return np.ascontiguousarray(wr.transpose(3, 0, 1, 2, 4).reshape(2, 16, 128, 4096))

    base = {
        "Wq": pre_cm(np.asarray(inputs["L0_attn_Wq"])),
        "Wk": np.stack([pre_cm(np.asarray(inputs["L0_attn_Wk"])[h]) for h in range(3)]),
        "Wv": np.stack([pre_cm(np.asarray(inputs["L0_attn_Wv"])[h]) for h in range(3)]),
        "Wo": pre_cm(np.asarray(inputs["L0_attn_Wo"])),
        "Wg": pre_cm(np.asarray(inputs["L0_attn_Wg"])),
        "dlog": np.asarray(inputs["L0_attn_decay_logits"]),
        "osc": np.asarray(inputs["L0_attn_out_scale"]).reshape(1),
        "wg8": pre_gc(np.asarray(inputs["L0_ffn_w_gate"])),
        "wu8": pre_gc(np.asarray(inputs["L0_ffn_w_up"])),
        "wd8": pre_dn(np.asarray(inputs["L0_ffn_w_down"])),
        "sg": np.asarray(inputs["L0_ffn_s_gate"]),
        "su": np.asarray(inputs["L0_ffn_s_up"]),
        "sd": np.asarray(inputs["L0_ffn_s_down"]),
        "W1": np.asarray(inputs["L1_attn_W"]),
        "dW1": np.asarray(inputs["L1_attn_delta_W"]),
        "g1": pre_gc(np.asarray(inputs["L1_ffn_w_gate"])),
        "u1": pre_gc(np.asarray(inputs["L1_ffn_w_up"])),
        "d1": pre_dn(np.asarray(inputs["L1_ffn_w_down"])),
    }
    return base


def kernel(**inputs):
    x = np.asarray(inputs["L0_attn_x"])
    xr = x.reshape(B, NBK, 128, D)
    nc = build_nc()
    base = host_base(inputs)
    in_maps = []
    for c in range(8):
        b, p = divmod(c, 2)
        own = xr[b, p::2].reshape(TOK, D)
        m = dict(base)
        m["xoT"] = np.ascontiguousarray(own.T)
        m["xsT"] = np.ascontiguousarray(x[b].T)
        m["pofs"] = np.asarray([128.0 * p], dtype=np.float32)
        in_maps.append(m)
    res = run_bass_kernel_spmd(nc, in_maps, core_ids=list(range(8)))
    out = np.empty((B, NBK, 128, D), dtype=np.float32)
    for c in range(8):
        b, p = divmod(c, 2)
        out[b, p::2] = res.results[c]["yT"].T.reshape(OWN, 128, D)
    return out.reshape(B, T, D)
